# version 13
# baseline (speedup 1.0000x reference)
import sys

import numpy as np

try:
    import concourse.bass as bass
except ImportError:
    sys.path.insert(0, "/opt/trn_rl_repo")
    import concourse.bass as bass
import concourse.mybir as mybir
from concourse.bass_utils import run_bass_kernel_spmd
from contextlib import ExitStack

F32 = mybir.dt.float32
BF16 = mybir.dt.bfloat16
AF = mybir.ActivationFunctionType
ALU = mybir.AluOpType
AX = mybir.AxisListType

S = 2048
E = 2048
H = 16
DN, DR, DV, R, QLR = 128, 64, 128, 512, 1536
SM = float((DN + DR) ** -0.5)
EPS_LN = 1e-5
EPS_RMS = 1e-6
NT = S // 128
NB = S // 512
NEG = -1.0e30

DEBUG = False
PHASES = 4
P1STAGE = 9


class Bld:
    ENGS = ("tensor", "vector", "scalar", "gpsimd", "sync")

    def __init__(self, nc):
        self.nc = nc
        self.q = {e: [] for e in self.ENGS}
        self.sems = {}
        self.cnt = {}
        self.hw = {e: {} for e in self.ENGS}

    def new_sem(self, st, name):
        self.sems[name] = st.enter_context(self.nc.semaphore(name))
        self.cnt[name] = 0
        return name

    def op(self, eng, fn, inc=None, amt=1):
        if inc is None:
            self.q[eng].append(lambda e, fn=fn: fn(e))
            return None
        h = self.sems[inc]
        self.q[eng].append(lambda e, fn=fn, h=h, a=amt: fn(e).then_inc(h, a))
        self.cnt[inc] += amt
        return self.cnt[inc]

    def wait(self, eng, sem, val):
        if val is None or val <= 0:
            return
        assert val <= self.cnt[sem], f"wait-on-future {sem} {val}>{self.cnt[sem]}"
        if self.hw[eng].get(sem, 0) >= val:
            return
        self.hw[eng][sem] = val
        h = self.sems[sem]
        self.q[eng].append(lambda e, h=h, v=val: e.wait_ge(h, v))

    def waits(self, eng, deps):
        best = {}
        for p in deps:
            if not p:
                continue
            if isinstance(p, list):
                items = p
            else:
                items = [p]
            for s, v in items:
                if v and (s not in best or v > best[s]):
                    best[s] = v
        for s, v in best.items():
            self.wait(eng, s, v)

    def flush(self, name):
        with self.nc.Block(name) as blk:
            for eng in self.ENGS:
                if self.q[eng]:
                    getattr(blk, eng)(lambda e, lst=list(self.q[eng]): [f(e) for f in lst])
        self.q = {e: [] for e in self.ENGS}



class DmaQ:

    def __init__(self, b, st, eng, name, k=8):
        self.b = b
        self.eng = eng
        self.k = k
        self.names = [b.new_sem(st, f"{name}{i}") for i in range(k)]
        self.n = 0

    def dma(self, fn):
        b = self.b
        i = self.n
        self.n += 1
        slot = i % self.k
        sem = self.names[slot]
        uses = i // self.k
        if uses > 0:
            b.wait(self.eng, sem, 16 * uses)
        cnt = b.op(self.eng, fn, sem, 16)
        return (sem, cnt)

    def drain(self):
        for sem in self.names:
            self.b.wait(self.eng, sem, self.b.cnt[sem])


def _dma3(dst, src3, c):
    return lambda e: e.dma_start(
        dst[:, :].rearrange("p (a c) -> p a c", c=c),
        src3.rearrange("a b c -> b a c"))


def _dma3_out(dst3, src, c):
    return lambda e: e.dma_start(
        dst3.rearrange("a b c -> b a c"),
        src[:, :].rearrange("p (a c) -> p a c", c=c))


def build_nc():
    nc = bass.Bass("TRN2", target_bir_lowering=False)

    x_d = nc.dram_tensor("x", [S, E], F32, kind="ExternalInput").ap()
    wqa = nc.dram_tensor("wqa", [16, 128, QLR], F32, kind="ExternalInput").ap()
    wkva = nc.dram_tensor("wkva", [16, 128, R + DR], F32, kind="ExternalInput").ap()
    wqbpe = nc.dram_tensor("wqbpe", [12, 128, H * DR], F32, kind="ExternalInput").ap()
    wqbnope = nc.dram_tensor("wqbnope", [H, 12, 128, DN], F32, kind="ExternalInput").ap()
    wukt = nc.dram_tensor("wukt", [H, 128, R], F32, kind="ExternalInput").ap()
    wuv = nc.dram_tensor("wuv", [H, 4, 128, DV], F32, kind="ExternalInput").ap()
    wo = nc.dram_tensor("wo", [4, H, 128, 512], F32, kind="ExternalInput").ap()
    wgup = nc.dram_tensor("wgup", [128, 16, 128, 128], F32, kind="ExternalInput").ap()
    wdown = nc.dram_tensor("wdown", [4, 64, 128, 512], F32, kind="ExternalInput").ap()
    lnq_d = nc.dram_tensor("lnq", [1, QLR], F32, kind="ExternalInput").ap()
    lnkv_d = nc.dram_tensor("lnkv", [1, R], F32, kind="ExternalInput").ap()
    cosx2_d = nc.dram_tensor("cosx2", [16, 128, 128], F32, kind="ExternalInput").ap()
    sinx2_d = nc.dram_tensor("sinx2", [16, 128, 128], F32, kind="ExternalInput").ap()
    ident_d = nc.dram_tensor("ident", [128, 128], F32, kind="ExternalInput").ap()
    mask_d = nc.dram_tensor("maskneg", [128, 128], F32, kind="ExternalInput").ap()
    ones_d = nc.dram_tensor("ones1", [1, 128], F32, kind="ExternalInput").ap()

    out_d = nc.dram_tensor("out", [S, E], F32, kind="ExternalOutput").ap()
    scratch_kind = "ExternalOutput" if DEBUG else "Internal"
    xnt_scr = nc.dram_tensor("xnt_scr", [16, 128, S], BF16, kind=scratch_kind).ap()
    qat_scr = nc.dram_tensor("qat_scr", [12, 128, S], BF16, kind=scratch_kind).ap()
    ya_scr = nc.dram_tensor("ya_scr", [S, E], F32, kind=scratch_kind).ap()
    wgup_bf = nc.dram_tensor("wgup_bf", [128, 16, 128, 128], BF16, kind="Internal").ap()
    wdown_bf = nc.dram_tensor("wdown_bf", [4, 64, 128, 512], BF16, kind="Internal").ap()

    b = Bld(nc)

    with ExitStack() as outer:
        sb = lambda n, sh, dt: outer.enter_context(nc.sbuf_tensor(n, sh, dt))
        SQ = DmaQ(b, outer, "sync", "Ds", 8)
        GQ = DmaQ(b, outer, "gpsimd", "Dg", 8)
        T = b.new_sem(outer, "T")
        V = b.new_sem(outer, "V")
        Sc = b.new_sem(outer, "Sc")

        PS = [outer.enter_context(nc.psum_tensor(f"ps{i}", [128, 512], F32)) for i in range(6)]
        PT = [outer.enter_context(nc.psum_tensor(f"pt{i}", [128, 1024], BF16)) for i in range(2)]
        ps_free = [None] * 6
        pt_free = [None] * 2

        ident = sb("ident_sb", [128, 128], BF16)
        maskneg = sb("maskneg_sb", [128, 128], F32)
        attres = ExitStack()
        asb = lambda n, sh, dt: attres.enter_context(nc.sbuf_tensor(n, sh, dt))
        kT = [asb(f"kT{i}", [128, S], BF16) for i in range(4)]
        kTpe_lo = asb("kTpe_lo", [128, S], BF16)
        kTpe_hi = asb("kTpe_hi", [128, S], BF16)
        ckv_nat = asb("ckv_nat", [128, 16 * 512], BF16)
        qpeT = [asb(f"qpeT{i}", [128, S], BF16) for i in range(8)]

        prep = ExitStack()
        psb0 = lambda n, sh, dt: prep.enter_context(nc.sbuf_tensor(n, sh, dt))
        cosx2 = psb0("cosx2_sb", [128, 16 * 128], F32)
        sinx2 = psb0("sinx2_sb", [128, 16 * 128], F32)
        lnq_rep = psb0("lnq_rep", [128, QLR], F32)
        lnkv_rep = psb0("lnkv_rep", [128, R], F32)
        lnq_row = psb0("lnq_row", [1, QLR], F32)
        lnkv_row = psb0("lnkv_row", [1, R], F32)
        ones_row = psb0("ones_row", [1, 128], F32)

        with ExitStack() as ph:
            psb = lambda n, sh, dt: ph.enter_context(nc.sbuf_tensor(n, sh, dt))
            x_t = [psb(f"x_t{i}", [128, E], F32) for i in range(2)]
            xn_t = [psb(f"xn_t{i}", [128, E], BF16) for i in range(2)]
            xnt_t = [psb(f"xnt_t{i}", [128, E], BF16) for i in range(2)]
            wkva_sb = psb("wkva_sb", [128, 16 * (R + DR)], BF16)
            stat6 = psb("stat6", [128, 24], F32)
            stat2 = psb("stat2", [128, 2], F32)
            inv_s = psb("inv_s", [128, 1], F32)
            nbias = psb("nbias", [128, 1], F32)
            rs_sb = psb("rs_sb", [128, 1], F32)
            vk_sb = psb("vk_sb", [128, 1], F32)
            invk = psb("invk", [128, 1], F32)
            ckv_f = psb("ckv_f", [128, R], F32)
            km1 = psb("km1", [128, 64], F32)
            km2 = psb("km2", [128, 64], F32)
            krop_lo = psb("krop_lo", [128, 128], BF16)
            krop_hi = psb("krop_hi", [128, 128], BF16)

            d_mask = SQ.dma(lambda e: e.dma_start(maskneg[:, :], mask_d[:, :]))
            d_lnq = SQ.dma(lambda e: e.dma_start(lnq_row[:, :], lnq_d[:, :]))
            d_lnkv = SQ.dma(lambda e: e.dma_start(lnkv_row[:, :], lnkv_d[:, :]))
            d_ones = SQ.dma(lambda e: e.dma_start(ones_row[:, :], ones_d[:, :]))
            d_cosA = SQ.dma(_dma3(cosx2, cosx2_d[:, :, :], 128))
            d_cosB = SQ.dma(_dma3(sinx2, sinx2_d[:, :, :], 128))
            d_cos = [d_cosA, d_cosB]
            d_ident = GQ.dma(lambda e: e.dma_start(ident[:, :], ident_d[:, :]))
            d_wkva = GQ.dma(_dma3(wkva_sb, wkva[:, :, :], 576))

            b.waits("tensor", [d_lnq, d_lnkv, d_ones])
            for j in range(3):
                b.op("tensor", lambda e, j=j: e.matmul(
                    PS[j][:, :], ones_row[:, :], lnq_row[:, j * 512:(j + 1) * 512],
                    start=True, stop=True), T if j == 2 else None)
            b.op("tensor", lambda e: e.matmul(
                PS[3][:, :], ones_row[:, :], lnkv_row[:, :], start=True, stop=True), T)
            t_rep = b.cnt[T]
            b.wait("vector", T, t_rep)
            for j in range(3):
                b.op("vector", lambda e, j=j: e.tensor_copy(
                    lnq_rep[:, j * 512:(j + 1) * 512], PS[j][:, :]))
            v_rep = b.op("vector", lambda e: e.tensor_copy(lnkv_rep[:, :], PS[3][:, :]), V)
            for j in range(4):
                ps_free[j] = (V, v_rep)

            b.op("vector", lambda e: e.memset(krop_lo[:, 64:128], 0.0))
            v_ms = b.op("vector", lambda e: e.memset(krop_hi[:, 0:64], 0.0), V)
            xnt_dma = [None] * NT
            x_free = [None, None]
            xn_free = [None, None]
            xnt_free = [None, None]
            ckv_ready = 0
            kT_ready = 0

            for tt in range(NT):
                sl = tt % 2
                b.waits("sync", [x_free[sl]])
                d_x = SQ.dma(lambda e, tt=tt, sl=sl: e.dma_start(
                    x_t[sl][:, :], x_d[tt * 128:(tt + 1) * 128, :]))
                b.waits("vector", [d_x])
                for sg in range(4):
                    b.op("vector", lambda e, sl=sl, sg=sg: e.bn_stats(
                        stat6[:, sg * 6:(sg + 1) * 6],
                        x_t[sl][:, sg * 512:(sg + 1) * 512]),
                        V if sg == 3 else None)
                v1 = b.cnt[V]
                b.wait("vector", V, v1)
                v2 = b.op("vector", lambda e: e.bn_aggr(stat2[:, :], stat6[:, :]), V)
                b.wait("vector", V, v2)
                v2b = b.op("vector", lambda e: e.tensor_scalar_add(
                    stat2[:, 1:2], stat2[:, 1:2], EPS_LN), V)
                b.wait("scalar", V, v2b)
                s1 = b.op("scalar", lambda e: e.activation(
                    inv_s[:, :], stat2[:, 1:2], AF.Ln), Sc)
                b.wait("scalar", Sc, s1)
                s2 = b.op("scalar", lambda e: e.activation(
                    inv_s[:, :], inv_s[:, :], AF.Exp, scale=-0.5), Sc)
                b.wait("vector", Sc, s2)
                v3 = b.op("vector", lambda e: e.scalar_tensor_tensor(
                    nbias[:, :], stat2[:, 0:1], -1.0, inv_s[:, :], ALU.mult, ALU.mult), V)
                b.waits("vector", [(V, v3), xn_free[sl]])
                s_xn = b.op("vector", lambda e, sl=sl: e.tensor_scalar(
                    xn_t[sl][:, :], x_t[sl][:, :], inv_s[:, :], nbias[:, :],
                    ALU.mult, ALU.add), V)
                x_free[sl] = [(V, s_xn)]
                if P1STAGE <= 1:
                    continue
                b.waits("tensor", [(V, s_xn), d_ident])
                b.waits("vector", [xnt_free[sl]])
                xnt_free[sl] = []
                for half in range(2):
                    pb = half
                    b.waits("tensor", [pt_free[pb]])
                    for j in range(8):
                        ec = half * 8 + j
                        b.op("tensor", lambda e, sl=sl, pb=pb, j=j, ec=ec: e.matmul(
                            PT[pb][:, j * 128:(j + 1) * 128],
                            xn_t[sl][:, ec * 128:(ec + 1) * 128], ident[:, :],
                            is_transpose=True, start=(j == 0), stop=(j == 7)),
                            T if j == 7 else None)
                    t_tr = b.cnt[T]
                    b.wait("vector", T, t_tr)
                    v_ev = b.op("vector", lambda e, sl=sl, half=half, pb=pb: e.tensor_copy(
                        xnt_t[sl][:, half * 1024:(half + 1) * 1024], PT[pb][:, :]), V)
                    pt_free[pb] = (V, v_ev)
                v_xnt = b.cnt[V]
                xn_free[sl] = (V, v_xnt)
                b.wait("sync", V, v_xnt)
                xnt_dma[tt] = SQ.dma((lambda sl, tt: _dma3_out(xnt_scr[:, :, tt * 128:(tt + 1) * 128], xnt_t[sl], 128))(sl, tt))
                if P1STAGE <= 2:
                    continue
                b.waits("tensor", [d_wkva, ps_free[4], ps_free[5], (V, v_xnt)])
                for ec in range(16):
                    b.op("tensor", lambda e, sl=sl, ec=ec: e.matmul(
                        PS[4][:, :], xnt_t[sl][:, ec * 128:(ec + 1) * 128],
                        wkva_sb[:, ec * 576:ec * 576 + 512],
                        start=(ec == 0), stop=(ec == 15)),
                        T if ec == 15 else None)
                t_ckv = b.cnt[T]
                for ec in range(16):
                    b.op("tensor", lambda e, sl=sl, ec=ec: e.matmul(
                        PS[5][:, 0:64], xnt_t[sl][:, ec * 128:(ec + 1) * 128],
                        wkva_sb[:, ec * 576 + 512:ec * 576 + 576],
                        start=(ec == 0), stop=(ec == 15)),
                        T if ec == 15 else None)
                t_kpe = b.cnt[T]
                xnt_free[sl] = [(T, t_kpe), xnt_dma[tt]]
                if P1STAGE <= 3:
                    continue
                b.wait("scalar", T, t_ckv)
                s3 = b.op("scalar", lambda e: e.square(ckv_f[:, :], PS[4][:, :]), Sc)
                b.wait("vector", Sc, s3)
                v4 = b.op("vector", lambda e: e.reduce_sum(rs_sb[:, :], ckv_f[:, :], AX.X), V)
                b.wait("vector", V, v4)
                v5 = b.op("vector", lambda e: e.tensor_scalar(
                    vk_sb[:, :], rs_sb[:, :], 1.0 / R, EPS_RMS, ALU.mult, ALU.add), V)
                b.wait("scalar", V, v5)
                s4 = b.op("scalar", lambda e: e.activation(invk[:, :], vk_sb[:, :], AF.Ln), Sc)
                b.wait("scalar", Sc, s4)
                s5 = b.op("scalar", lambda e: e.activation(
                    invk[:, :], invk[:, :], AF.Exp, scale=-0.5), Sc)
                b.wait("scalar", Sc, s5)
                s6 = b.op("scalar", lambda e: e.activation(
                    ckv_f[:, :], PS[4][:, :], AF.Copy, scale=invk[:, :]), Sc)
                ps_free[4] = (Sc, s6)
                b.wait("vector", Sc, s6)
                v6 = b.op("vector", lambda e, tt=tt: e.tensor_tensor(
                    ckv_nat[:, tt * 512:(tt + 1) * 512], ckv_f[:, :], lnkv_rep[:, :],
                    ALU.mult), V)
                ckv_ready = v6
                if P1STAGE <= 4:
                    continue
                b.waits("vector", [(T, t_kpe)] + d_cos)
                b.op("vector", lambda e, tt=tt: e.tensor_tensor(
                    km1[:, :], PS[5][:, 0:64], cosx2[:, tt * 128:tt * 128 + 64], ALU.mult))
                b.op("vector", lambda e, tt=tt: e.tensor_tensor(
                    km2[:, 0:32], PS[5][:, 32:64], sinx2[:, tt * 128:tt * 128 + 32], ALU.mult))
                v7 = b.op("vector", lambda e, tt=tt: e.tensor_tensor(
                    km2[:, 32:64], PS[5][:, 0:32], sinx2[:, tt * 128 + 32:tt * 128 + 64],
                    ALU.mult), V)
                b.wait("vector", V, v7)
                b.op("vector", lambda e: e.tensor_tensor(
                    krop_lo[:, 0:64], km1[:, :], km2[:, :], ALU.add))
                v8 = b.op("vector", lambda e: e.tensor_tensor(
                    krop_hi[:, 64:128], km1[:, :], km2[:, :], ALU.add), V)
                ps_free[5] = (V, v8)
                if P1STAGE <= 5:
                    continue
                b.waits("tensor", [(V, v6), (V, v8), (V, v_ms), pt_free[0]])
                for rc in range(4):
                    b.op("tensor", lambda e, tt=tt, rc=rc: e.matmul(
                        PT[0][:, rc * 128:(rc + 1) * 128],
                        ckv_nat[:, tt * 512 + rc * 128:tt * 512 + (rc + 1) * 128],
                        ident[:, :], is_transpose=True, start=(rc == 0), stop=False))
                b.op("tensor", lambda e: e.matmul(
                    PT[0][:, 512:640], krop_lo[:, :], ident[:, :],
                    is_transpose=True, start=False, stop=False))
                t_k5 = b.op("tensor", lambda e: e.matmul(
                    PT[0][:, 640:768], krop_hi[:, :], ident[:, :],
                    is_transpose=True, start=False, stop=True), T)
                b.wait("vector", T, t_k5)
                for rc in range(4):
                    b.op("vector", lambda e, tt=tt, rc=rc: e.tensor_copy(
                        kT[rc][:, tt * 128:(tt + 1) * 128], PT[0][:, rc * 128:(rc + 1) * 128]))
                b.op("vector", lambda e, tt=tt: e.tensor_copy(
                    kTpe_lo[:, tt * 128:(tt + 1) * 128], PT[0][:, 512:640]))
                v9 = b.op("vector", lambda e, tt=tt: e.tensor_copy(
                    kTpe_hi[:, tt * 128:(tt + 1) * 128], PT[0][:, 640:768]), V)
                pt_free[0] = (V, v9)
                kT_ready = v9

            if PHASES <= 1:
                SQ.drain()
                GQ.drain()
            SQ.drain()
            GQ.drain()
            b.flush("P1")
        if PHASES <= 1:
            prep.close()
            attres.close()
            return nc

        with ExitStack() as ph:
            psb = lambda n, sh, dt: ph.enter_context(nc.sbuf_tensor(n, sh, dt))
            xnt_t = [psb(f"p2xnt{i}", [128, E], BF16) for i in range(2)]
            wqa_sb = psb("wqa_sb", [128, 16 * QLR], BF16)
            wqbpe_sb = psb("wqbpe_sb", [128, 12 * 1024], BF16)
            qa_f = psb("qa_f", [128, QLR], F32)
            qa_bf = psb("qa_bf", [128, QLR], BF16)
            qat_t = [psb(f"qat_t{i}", [128, 12 * 128], BF16) for i in range(2)]
            qpe_bf = psb("qpe_bf", [128, 1024], BF16)
            rs2 = psb("rs2", [128, 1], F32)
            vq2 = psb("vq2", [128, 1], F32)
            invq = psb("invq", [128, 1], F32)
            qm1 = psb("qm1", [128, 256], F32)
            qm2 = psb("qm2", [128, 256], F32)

            d_wqa = GQ.dma(_dma3(wqa_sb, wqa[:, :, :], 1536))
            d_wpe = GQ.dma(_dma3(wqbpe_sb, wqbpe[:, :, :], 1024))
            xnt_free2 = [None, None]
            qat_free = [None, None]
            qat_dma = [None] * NT
            qpeT_ready = 0

            for tt in range(NT):
                sl = tt % 2
                b.waits("sync", [xnt_free2[sl]])
                b.waits("sync", [xnt_dma[tt]])
                d_xn = SQ.dma((lambda sl, tt: _dma3(xnt_t[sl], xnt_scr[:, :, tt * 128:(tt + 1) * 128], 128))(sl, tt))
                b.waits("tensor", [d_wqa, d_xn,
                                   ps_free[0], ps_free[1], ps_free[2]])
                for lg in range(3):
                    for ec in range(16):
                        b.op("tensor", lambda e, sl=sl, lg=lg, ec=ec: e.matmul(
                            PS[lg][:, :], xnt_t[sl][:, ec * 128:(ec + 1) * 128],
                            wqa_sb[:, ec * QLR + lg * 512:ec * QLR + (lg + 1) * 512],
                            start=(ec == 0), stop=(ec == 15)),
                            T if ec == 15 else None)
                t_qa = b.cnt[T]
                xnt_free2[sl] = (T, t_qa)
                b.wait("scalar", T, t_qa)
                for lg in range(3):
                    b.op("scalar", lambda e, lg=lg: e.square(
                        qa_f[:, lg * 512:(lg + 1) * 512], PS[lg][:, :]),
                        Sc if lg == 2 else None)
                s1 = b.cnt[Sc]
                b.wait("vector", Sc, s1)
                v1 = b.op("vector", lambda e: e.reduce_sum(rs2[:, :], qa_f[:, :], AX.X), V)
                b.wait("vector", V, v1)
                v2 = b.op("vector", lambda e: e.tensor_scalar(
                    vq2[:, :], rs2[:, :], 1.0 / QLR, EPS_RMS, ALU.mult, ALU.add), V)
                b.wait("scalar", V, v2)
                s2 = b.op("scalar", lambda e: e.activation(invq[:, :], vq2[:, :], AF.Ln), Sc)
                b.wait("scalar", Sc, s2)
                s3 = b.op("scalar", lambda e: e.activation(
                    invq[:, :], invq[:, :], AF.Exp, scale=-0.5), Sc)
                b.wait("scalar", Sc, s3)
                for lg in range(3):
                    b.op("scalar", lambda e, lg=lg: e.activation(
                        qa_f[:, lg * 512:(lg + 1) * 512], PS[lg][:, :], AF.Copy,
                        scale=invq[:, :]), Sc if lg == 2 else None)
                s4 = b.cnt[Sc]
                for lg in range(3):
                    ps_free[lg] = (Sc, s4)
                b.wait("vector", Sc, s4)
                v3 = b.op("vector", lambda e: e.tensor_tensor(
                    qa_bf[:, :], qa_f[:, :], lnq_rep[:, :], ALU.mult), V)
                b.waits("tensor", [(V, v3), pt_free[0], pt_free[1], qat_free[sl]])
                for lc in range(12):
                    pb = lc // 8
                    b.op("tensor", lambda e, lc=lc, pb=pb: e.matmul(
                        PT[pb][:, (lc % 8) * 128:((lc % 8) + 1) * 128],
                        qa_bf[:, lc * 128:(lc + 1) * 128], ident[:, :],
                        is_transpose=True, start=(lc % 8 == 0),
                        stop=(lc in (7, 11))),
                        T if lc in (7, 11) else None)
                t_tr = b.cnt[T]
                b.wait("vector", T, t_tr)
                b.op("vector", lambda e, sl=sl: e.tensor_copy(
                    qat_t[sl][:, 0:1024], PT[0][:, :]))
                v4 = b.op("vector", lambda e, sl=sl: e.tensor_copy(
                    qat_t[sl][:, 1024:1536], PT[1][:, 0:512]), V)
                pt_free[0] = (V, v4)
                pt_free[1] = (V, v4)
                b.wait("sync", V, v4)
                d_qa = SQ.dma((lambda sl, tt: _dma3_out(qat_scr[:, :, tt * 128:(tt + 1) * 128], qat_t[sl], 128))(sl, tt))
                b.waits("tensor", [d_wpe, (V, v4), ps_free[3], ps_free[4]])
                for ng in range(2):
                    for lc in range(12):
                        b.op("tensor", lambda e, sl=sl, ng=ng, lc=lc: e.matmul(
                            PS[3 + ng][:, :], qat_t[sl][:, lc * 128:(lc + 1) * 128],
                            wqbpe_sb[:, lc * 1024 + ng * 512:lc * 1024 + (ng + 1) * 512],
                            start=(lc == 0), stop=(lc == 11)),
                            T if lc == 11 else None)
                t_pe = b.cnt[T]
                qat_free[sl] = [(T, t_pe), d_qa]
                qat_dma[tt] = d_qa
                b.waits("vector", [(T, t_pe)])
                for hp in range(8):
                    bank = PS[3 + hp // 4]
                    o = (hp % 4) * 128
                    q0 = (hp % 2) * 128
                    b.op("vector", lambda e, bank=bank, o=o, tt=tt, q0=q0: e.tensor_tensor(
                        qm1[:, q0:q0 + 128], bank[:, o:o + 128],
                        cosx2[:, tt * 128:(tt + 1) * 128], ALU.mult))
                    b.op("vector", lambda e, bank=bank, o=o, tt=tt, q0=q0: e.tensor_tensor(
                        qm2[:, q0:q0 + 32], bank[:, o + 32:o + 64],
                        sinx2[:, tt * 128:tt * 128 + 32], ALU.mult))
                    b.op("vector", lambda e, bank=bank, o=o, tt=tt, q0=q0: e.tensor_tensor(
                        qm2[:, q0 + 32:q0 + 64], bank[:, o:o + 32],
                        sinx2[:, tt * 128 + 32:tt * 128 + 64], ALU.mult))
                    b.op("vector", lambda e, bank=bank, o=o, tt=tt, q0=q0: e.tensor_tensor(
                        qm2[:, q0 + 64:q0 + 96], bank[:, o + 96:o + 128],
                        sinx2[:, tt * 128 + 64:tt * 128 + 96], ALU.mult))
                    v5 = b.op("vector", lambda e, bank=bank, o=o, tt=tt, q0=q0: e.tensor_tensor(
                        qm2[:, q0 + 96:q0 + 128], bank[:, o + 64:o + 96],
                        sinx2[:, tt * 128 + 96:tt * 128 + 128], ALU.mult), V)
                    b.wait("vector", V, v5)
                    b.op("vector", lambda e, hp=hp, q0=q0: e.tensor_tensor(
                        qpe_bf[:, hp * 128:(hp + 1) * 128], qm1[:, q0:q0 + 128],
                        qm2[:, q0:q0 + 128], ALU.add),
                        V if hp == 7 else None)
                v6 = b.cnt[V]
                ps_free[3] = (V, v6)
                ps_free[4] = (V, v6)
                b.waits("tensor", [(V, v6), pt_free[0]])
                for hp in range(8):
                    b.op("tensor", lambda e, hp=hp: e.transpose(
                        PT[0][:, hp * 128:(hp + 1) * 128],
                        qpe_bf[:, hp * 128:(hp + 1) * 128], ident[:, :]),
                        T if hp == 7 else None)
                t_tr2 = b.cnt[T]
                b.wait("vector", T, t_tr2)
                for hp in range(8):
                    b.op("vector", lambda e, hp=hp, tt=tt: e.tensor_copy(
                        qpeT[hp][:, tt * 128:(tt + 1) * 128],
                        PT[0][:, hp * 128:(hp + 1) * 128]),
                        V if hp == 7 else None)
                qpeT_ready = b.cnt[V]
                pt_free[0] = (V, qpeT_ready)

            if PHASES <= 2:
                SQ.drain()
                GQ.drain()
            SQ.drain()
            GQ.drain()
            b.flush("P2")
        if PHASES <= 2:
            prep.close()
            attres.close()
            return nc

        prep.close()

        with ExitStack() as ph:
            psb = lambda n, sh, dt: ph.enter_context(nc.sbuf_tensor(n, sh, dt))
            qat_B = psb("qat_B", [128, 12 * 512], BF16)
            wukt_sb = [psb(f"wukt{i}", [128, 512], BF16) for i in range(2)]
            wnope_sb = [psb(f"wnope{i}", [128, 12 * 128], BF16) for i in range(2)]
            wuv_sb = [psb(f"wuv{i}", [128, 4 * 128], BF16) for i in range(2)]
            wo_sb = [psb(f"wo{i}", [128, H * 512], BF16) for i in range(2)]
            ptasm = psb("ptasm", [128, 16 * 512], BF16)
            p_nat = [psb(f"p_nat{i}", [128, S], BF16) for i in range(2)]
            qf = [psb(f"qf{i}", [128, 4 * 512], BF16) for i in range(2)]
            qnope_sb = [psb(f"qnope{i}", [128, 512], BF16) for i in range(2)]
            o_sb = psb("o_sb", [128, 4 * 512], BF16)
            t1_sb = psb("t1_sb", [128, H * 512], BF16)
            s_stage = psb("s_stage", [128, S], F32)
            y_sb = [psb(f"y_sb{i}", [128, 512], F32) for i in range(2)]
            mx_part = psb("mx_part", [128, 4], F32)
            mx_row = psb("mx_row", [128, 1], F32)
            e_bias = psb("e_bias", [128, 1], F32)
            sm_sums = psb("sm_sums", [128, 4], F32)
            sm_tot = psb("sm_tot", [128, 1], F32)
            sm_inv = psb("sm_inv", [128, 1], F32)

            b.waits("vector", [d_mask])

            conv_sb = [psb(f"conv{i}", [128, 2048], BF16) for i in range(3)]
            conv_free = [None] * 3
            conv_units = ([("g", oc) for oc in range(128)]
                          + [("d", u) for u in range(64)])
            conv_i = [0]

            def emit_conversions(n):
                for _ in range(n):
                    if conv_i[0] >= len(conv_units):
                        return
                    kind, u = conv_units[conv_i[0]]
                    slot = conv_i[0] % 3
                    conv_i[0] += 1
                    b.waits("gpsimd", [conv_free[slot]])
                    if kind == "g":
                        tok = GQ.dma(_dma3(conv_sb[slot], wgup[u, :, :, :], 128))
                        b.waits("sync", [tok])
                        st = SQ.dma((lambda u, slot: _dma3_out(
                            wgup_bf[u, :, :, :], conv_sb[slot], 128))(u, slot))
                    else:
                        eg, ic0 = u % 4, (u // 4) * 4
                        tok = GQ.dma((lambda eg, ic0, slot: _dma3(
                            conv_sb[slot], wdown[eg, ic0:ic0 + 4, :, :], 512))(eg, ic0, slot))
                        b.waits("sync", [tok])
                        st = SQ.dma((lambda eg, ic0, slot: _dma3_out(
                            wdown_bf[eg, ic0:ic0 + 4, :, :], conv_sb[slot], 512))(eg, ic0, slot))
                    conv_free[slot] = st

            qat_B_free = [None]
            wnope_free = [None, None]
            wuv_free = [None, None]
            wo_free = [None, None]
            ptasm_free = None
            p_nat_free = [None, None]
            qf_free = [None, None]
            qnope_free = [None, None]
            o_sb_free = None
            t1_free = None
            y_free = [None, None]
            ya_dma = None
            ya_done = [None] * NB
            d_nope = [None, None]
            d_uv = [None, None]
            d_wo = [None, None]

            def load_nope(h, slot):
                b.waits("gpsimd", [wnope_free[slot]])
                d_nope[slot] = GQ.dma((lambda h, slot: _dma3(wnope_sb[slot], wqbnope[h, :, :, :], 128))(h, slot))

            wukt_free = [None, None]
            d_ukt = [None, None]

            def load_ukt(h, slot):
                b.waits("gpsimd", [wukt_free[slot]])
                d_ukt[slot] = GQ.dma((lambda h, slot: e_dma_ukt(h, slot))(h, slot))

            def e_dma_ukt(h, slot):
                return lambda e: e.dma_start(wukt_sb[slot][:, :], wukt[h, :, :])

            def load_uv(h, slot):
                b.waits("gpsimd", [wuv_free[slot]])
                d_uv[slot] = GQ.dma((lambda h, slot: _dma3(wuv_sb[slot], wuv[h, :, :, :], 128))(h, slot))

            def load_wo(eg, slot):
                b.waits("gpsimd", [wo_free[slot]])
                d_wo[slot] = GQ.dma((lambda eg, slot: _dma3(wo_sb[slot], wo[eg, :, :, :], 512))(eg, slot))

            load_nope(0, 0)
            load_uv(0, 0)
            load_ukt(0, 0)

            d_qat = [None]
            for B in range(NB):
                nkB = 4 * (B + 1)
                b.waits("sync", [qat_B_free[0]])
                b.waits("sync", [qat_dma[4 * B + j] for j in range(4)])
                d_qat[0] = SQ.dma((lambda B: _dma3(
                    qat_B, qat_scr[:, :, B * 512:(B + 1) * 512], 512))(B))
                if B == 0:
                    load_wo(0, 0)
                def emit_qnope(h2, B=B):
                    hs2 = h2 % 2
                    b.waits("tensor", [d_nope[hs2], d_qat[0], ps_free[5]])
                    for lc in range(12):
                        b.op("tensor", lambda e, hs2=hs2, lc=lc: e.matmul(
                            PS[5][:, :], wnope_sb[hs2][:, lc * 128:(lc + 1) * 128],
                            qat_B[:, lc * 512:(lc + 1) * 512],
                            start=(lc == 0), stop=(lc == 11)),
                            T if lc == 11 else None)
                    t_qn = b.cnt[T]
                    if h2 < 15:
                        load_nope(h2 + 1, 1 - hs2)
                    elif B < NB - 1:
                        load_nope(0, 1 - hs2)
                    wnope_free[hs2] = (T, t_qn)
                    b.waits("scalar", [(T, t_qn), qnope_free[hs2]])
                    s_qn = b.op("scalar", lambda e, hs2=hs2: e.copy(
                        qnope_sb[hs2][:, :], PS[5][:, :]), Sc)
                    ps_free[5] = (Sc, s_qn)
                    return s_qn

                s_qn_tok = [None, None]
                for h in range(16):
                    hs = h % 2
                    emit_conversions(3)
                    if h == 0:
                        s_qn_tok[0] = emit_qnope(0)
                    s_qn = s_qn_tok[hs]
                    qab_banks = (3, 4, 5, 0)
                    b.waits("tensor", [d_ukt[hs], (Sc, s_qn)]
                            + [ps_free[j] for j in qab_banks])
                    for rc in range(4):
                        b.op("tensor", lambda e, hs=hs, rc=rc, qb=qab_banks: e.matmul(
                            PS[qb[rc]][:, :],
                            wukt_sb[hs][:, rc * 128:(rc + 1) * 128],
                            qnope_sb[hs][:, :], start=True, stop=True),
                            T if rc == 3 else None)
                    t_qab = b.cnt[T]
                    wukt_free[hs] = (T, t_qab)
                    if h < 15:
                        load_ukt(h + 1, 1 - hs)
                    elif B < NB - 1:
                        load_ukt(0, 1 - hs)
                    qnope_free[hs] = (T, t_qab)
                    b.waits("vector", [(T, t_qab), qf_free[hs]])
                    for rc in range(4):
                        b.op("vector", lambda e, hs=hs, rc=rc, qb=qab_banks: e.tensor_copy(
                            qf[hs][:, rc * 512:(rc + 1) * 512], PS[qb[rc]][:, :]),
                            V if rc == 3 else None)
                    v_qf = b.cnt[V]
                    for j in qab_banks:
                        ps_free[j] = (V, v_qf)

                    pt_evac_v = 0
                    last_S_T = 0
                    sbank = 0
                    prev_exp = None
                    pend_tr = None

                    def emit_transposes(i, n_k, v_pn):
                        nonlocal pt_evac_v
                        b.wait("tensor", V, v_pn)
                        t_tr_done = 0
                        for kc in range(n_k):
                            pb = (kc // 8) % 2
                            slot = kc % 8
                            if slot == 0:
                                b.waits("tensor", [pt_free[pb]])
                            b.op("tensor", lambda e, pb=pb, slot=slot, kc=kc, i=i,
                                 last=(slot == 7 or kc == n_k - 1): e.matmul(
                                PT[pb][:, slot * 128:(slot + 1) * 128],
                                p_nat[i % 2][:, kc * 128:(kc + 1) * 128], ident[:, :],
                                is_transpose=True, start=(slot == 0), stop=last),
                                T if (slot == 7 or kc == n_k - 1) else None)
                            if slot == 7 or kc == n_k - 1:
                                t_tr_done = b.cnt[T]
                                nsl = slot + 1
                                base = kc - slot
                                b.waits("vector", [(T, t_tr_done), ptasm_free])
                                b.op("vector", lambda e, pb=pb, nsl=nsl, base=base, i=i:
                                     e.tensor_copy(
                                         ptasm[:, :].rearrange(
                                             "p (a c) -> p a c", c=512)[
                                             :, base:base + nsl, i * 128:(i + 1) * 128],
                                         PT[pb][:, :].rearrange(
                                             "p (a c) -> p a c", c=128)[:, 0:nsl, :]),
                                     V)
                                pt_free[pb] = (V, b.cnt[V])
                        p_nat_free[i % 2] = (T, t_tr_done)
                        pt_evac_v = b.cnt[V]

                    for i in range(4):
                        qi = 4 * B + i
                        n_k = qi + 1
                        G = (n_k + 3) // 4
                        s_ev = {}
                        for g in range(G):
                            w = (min(4 * g + 4, n_k) - 4 * g) * 128
                            pk = 1 + (sbank % 3)
                            sbank += 1
                            bank = PS[pk]
                            b.waits("tensor", [(V, v_qf), ps_free[pk],
                                               (V, qpeT_ready), (V, kT_ready)])
                            for cc in range(5):
                                if cc < 4:
                                    def lhs(e, hs=hs, cc=cc, i=i):
                                        return qf[hs][:, cc * 512 + i * 128:cc * 512 + (i + 1) * 128]

                                    def rhs(e, cc=cc, g=g, w=w):
                                        return kT[cc][:, 4 * g * 128:4 * g * 128 + w]
                                else:
                                    def lhs(e, h=h, B=B, i=i):
                                        return qpeT[h // 2][:,
                                                            B * 512 + i * 128:B * 512 + (i + 1) * 128]

                                    def rhs(e, h=h, g=g, w=w):
                                        kk = kTpe_lo if h % 2 == 0 else kTpe_hi
                                        return kk[:, 4 * g * 128:4 * g * 128 + w]
                                b.op("tensor", lambda e, bank=bank, w=w, cc=cc,
                                     lhs=lhs, rhs=rhs: e.matmul(
                                         bank[:, 0:w], lhs(e), rhs(e),
                                         start=(cc == 0), stop=(cc == 4)),
                                     T if cc == 4 else None)
                            t_sg = b.cnt[T]
                            b.wait("scalar", T, t_sg)
                            if prev_exp is not None:
                                b.wait("scalar", Sc, prev_exp)
                            s_ev[g] = b.op("scalar", lambda e, g=g, w=w, bank=bank: e.copy(
                                s_stage[:, 4 * g * 128:4 * g * 128 + w], bank[:, 0:w]), Sc)
                            ps_free[pk] = (Sc, s_ev[g])
                        last_S_T = b.cnt[T]
                        gl = G - 1
                        doff = (qi) * 128
                        b.wait("vector", Sc, s_ev[gl])
                        v_mask = b.op("vector", lambda e, doff=doff: e.tensor_tensor(
                            s_stage[:, doff:doff + 128], s_stage[:, doff:doff + 128],
                            maskneg[:, :], ALU.add), V)
                        for g in range(G):
                            w = (min(4 * g + 4, n_k) - 4 * g) * 128
                            b.wait("vector", Sc, s_ev[g])
                            if g == gl:
                                b.wait("vector", V, v_mask)
                            b.op("vector", lambda e, g=g, w=w: e.reduce_max(
                                mx_part[:, g:g + 1], s_stage[:, 4 * g * 128:4 * g * 128 + w],
                                AX.X), V if g == G - 1 else None)
                        v_pm = b.cnt[V]
                        b.wait("vector", V, v_pm)
                        v_cm = b.op("vector", lambda e, G=G: e.reduce_max(
                            mx_row[:, :], mx_part[:, 0:G], AX.X), V)
                        b.wait("vector", V, v_cm)
                        v_bias = b.op("vector", lambda e: e.tensor_scalar(
                            e_bias[:, :], mx_row[:, :], -SM, None, ALU.mult), V)
                        b.waits("scalar", [(V, v_bias), p_nat_free[i % 2]])
                        for g in range(G):
                            w = (min(4 * g + 4, n_k) - 4 * g) * 128
                            b.op("scalar", lambda e, g=g, w=w, i=i: e.activation(
                                p_nat[i % 2][:, 4 * g * 128:4 * g * 128 + w],
                                s_stage[:, 4 * g * 128:4 * g * 128 + w],
                                AF.Exp, bias=e_bias[:, :], scale=SM,
                                accum_out=sm_sums[:, g:g + 1]),
                                Sc if g == G - 1 else None)
                        s_exp = b.cnt[Sc]
                        prev_exp = s_exp
                        b.wait("vector", Sc, s_exp)
                        v_st = b.op("vector", lambda e, G=G: e.reduce_sum(
                            sm_tot[:, :], sm_sums[:, 0:G], AX.X), V)
                        b.wait("vector", V, v_st)
                        v_si = b.op("vector", lambda e: e.reciprocal(
                            sm_inv[:, :], sm_tot[:, :]), V)
                        b.wait("vector", V, v_si)
                        v_pn = b.op("vector", lambda e, n_k=n_k, i=i: e.tensor_scalar(
                            p_nat[i % 2][:, 0:n_k * 128], p_nat[i % 2][:, 0:n_k * 128],
                            sm_inv[:, :], None, ALU.mult), V)
                        if pend_tr is not None:
                            emit_transposes(*pend_tr)
                        pend_tr = (i, n_k, v_pn)
                    if h < 15:
                        s_qn_tok[1 - hs] = emit_qnope(h + 1)
                    emit_transposes(*pend_tr)
                    qf_free[hs] = (T, last_S_T)

                    o_banks = (3, 4, 5, 0)
                    b.waits("tensor", [(V, pt_evac_v), (V, ckv_ready),
                                       t1_free if h == 0 else None]
                            + [ps_free[j] for j in o_banks])
                    for kc in range(nkB):
                        j0 = max(0, kc - 4 * B)
                        for rc in range(4):
                            b.op("tensor", lambda e, kc=kc, j0=j0, rc=rc, ob=o_banks,
                                 nkB=nkB: e.matmul(
                                     PS[ob[rc]][:, j0 * 128:512],
                                     ckv_nat[:, kc * 512 + rc * 128:kc * 512 + (rc + 1) * 128],
                                     ptasm[:, kc * 512 + j0 * 128:(kc + 1) * 512],
                                     start=(kc == 0), stop=(kc == nkB - 1)),
                                 T if (kc == nkB - 1 and rc == 3) else None)
                    t_o = b.cnt[T]
                    ptasm_free = (T, t_o)
                    b.waits("vector", [(T, t_o), o_sb_free])
                    b.waits("scalar", [(T, t_o), o_sb_free])
                    for rc in range(2):
                        b.op("vector", lambda e, rc=rc, ob=o_banks: e.tensor_copy(
                            o_sb[:, rc * 512:(rc + 1) * 512], PS[ob[rc]][:, :]),
                            V if rc == 1 else None)
                    v_o = b.cnt[V]
                    for rc in range(2, 4):
                        b.op("scalar", lambda e, rc=rc, ob=o_banks: e.copy(
                            o_sb[:, rc * 512:(rc + 1) * 512], PS[ob[rc]][:, :]),
                            Sc if rc == 3 else None)
                    s_o = b.cnt[Sc]
                    ps_free[o_banks[0]] = (V, v_o)
                    ps_free[o_banks[1]] = (V, v_o)
                    ps_free[o_banks[2]] = (Sc, s_o)
                    ps_free[o_banks[3]] = (Sc, s_o)
                    b.waits("tensor", [(V, v_o), (Sc, s_o), d_uv[hs], ps_free[1]])
                    for rc in range(4):
                        b.op("tensor", lambda e, hs=hs, rc=rc: e.matmul(
                            PS[1][:, :], wuv_sb[hs][:, rc * 128:(rc + 1) * 128],
                            o_sb[:, rc * 512:(rc + 1) * 512],
                            start=(rc == 0), stop=(rc == 3)),
                            T if rc == 3 else None)
                    t_t1 = b.cnt[T]
                    o_sb_free = (T, t_t1)
                    wuv_free[hs] = (T, t_t1)
                    if h < 15:
                        load_uv(h + 1, 1 - hs)
                    elif B < NB - 1:
                        load_uv(0, 1 - hs)
                    b.wait("scalar", T, t_t1)
                    s_t1 = b.op("scalar", lambda e, h=h: e.copy(
                        t1_sb[:, h * 512:(h + 1) * 512], PS[1][:, :]), Sc)
                    ps_free[1] = (Sc, s_t1)
                t1_all_ready = b.cnt[Sc]

                for eg in range(4):
                    egs = eg % 2
                    if eg < 3:
                        load_wo(eg + 1, 1 - egs)
                    elif B < NB - 1:
                        load_wo(0, 1 - egs)
                    for tt4 in range(4):
                        bank = PS[1 + (tt4 % 2)]
                        b.waits("tensor", [d_wo[egs], (Sc, t1_all_ready),
                                           ps_free[1 + (tt4 % 2)]])
                        for hh in range(16):
                            b.op("tensor", lambda e, hh=hh, tt4=tt4, egs=egs, bank=bank:
                                 e.matmul(
                                     bank[:, :],
                                     t1_sb[:, hh * 512 + tt4 * 128:hh * 512 + (tt4 + 1) * 128],
                                     wo_sb[egs][:, hh * 512:(hh + 1) * 512],
                                     start=(hh == 0), stop=(hh == 15)),
                                 T if hh == 15 else None)
                        t_y = b.cnt[T]
                        if tt4 == 3:
                            wo_free[egs] = (T, t_y)
                        ysl = tt4 % 2
                        b.waits("scalar", [(T, t_y), y_free[ysl]])
                        s_y = b.op("scalar", lambda e, ysl=ysl, bank=bank: e.copy(
                            y_sb[ysl][:, :], bank[:, :]), Sc)
                        ps_free[1 + (tt4 % 2)] = (Sc, s_y)
                        b.wait("sync", Sc, s_y)
                        ya_dma = SQ.dma(lambda e, B=B, tt4=tt4, eg=eg, ysl=ysl:
                                      e.dma_start(
                                          ya_scr[B * 512 + tt4 * 128:B * 512 + (tt4 + 1) * 128,
                                                 eg * 512:(eg + 1) * 512],
                                          y_sb[ysl][:, :]))
                        y_free[ysl] = ya_dma
                qat_B_free[0] = (T, b.cnt[T])
                ya_done[B] = ya_dma
                t1_free = (T, b.cnt[T])

            if PHASES <= 3:
                SQ.drain()
                GQ.drain()
            SQ.drain()
            GQ.drain()
            b.flush("ATT")
        attres.close()
        if PHASES <= 3:
            return nc

        with ExitStack() as ph:
            psb = lambda n, sh, dt: ph.enter_context(nc.sbuf_tensor(n, sh, dt))
            xnt_g = [psb(f"fx{i}", [128, 16 * 512], BF16) for i in range(2)]
            h_sb = psb("h_sb", [128, 64 * 512], BF16)
            wg_sb = [psb(f"wg{i}", [128, 16 * 128], BF16) for i in range(4)]
            wd_sb = [psb(f"wd{i}", [128, 4 * 512], BF16) for i in range(2)]
            sil_sb = [psb(f"sil{i}", [128, 512], F32) for i in range(2)]
            xr_sb = psb("xr_sb", [128, 16 * 512], F32)
            yr_sb = psb("yr_sb", [128, 16 * 512], F32)
            ydn_sb = psb("ydn_sb", [128, 4 * 512], F32)
            out_sb = [psb(f"os{i}", [128, 512], F32) for i in range(2)]

            xnt_g_free = [None, None]
            wg_free = [None] * 4
            wd_free = [None, None]
            sil_free = [None, None]
            xr_free = None
            yr_free = None
            ydn_free = None
            d_xr = None
            d_yr = None
            out_free = [None, None]
            h_free_T = 0
            d_g = [None] * 4
            d_d = [None, None]
            out_dma = None

            def load_gate_tile(j, kind, slot):
                oc = j + (64 if kind else 0)
                b.waits("sync", [wg_free[slot]])
                d_g[slot] = SQ.dma((lambda oc, slot: _dma3(wg_sb[slot], wgup_bf[oc, :, :, :], 128))(oc, slot))

            def load_down(eg, ic0, slot):
                b.waits("sync", [wd_free[slot]])
                d_d[slot] = SQ.dma((lambda eg, ic0, slot: _dma3(wd_sb[slot], wdown_bf[eg, ic0:ic0 + 4, :, :], 512))(eg, ic0, slot))

            for g in range(4):
                gs = g % 2
                b.waits("sync", [xnt_g_free[gs]])
                b.waits("sync", list(xnt_dma))
                d_xg = SQ.dma((lambda g, gs: _dma3(xnt_g[gs], xnt_scr[:, :, g * 512:(g + 1) * 512], 512))(g, gs))
                if g == 0:
                    load_gate_tile(0, 0, 0)
                    load_gate_tile(0, 1, 1)
                for j in range(64):
                    s_g = 2 * j % 4
                    s_u = (2 * j + 1) % 4
                    if j < 63:
                        load_gate_tile(j + 1, 0, (2 * j + 2) % 4)
                        load_gate_tile(j + 1, 1, (2 * j + 3) % 4)
                    elif g < 3:
                        load_gate_tile(0, 0, (2 * j + 2) % 4)
                        load_gate_tile(0, 1, (2 * j + 3) % 4)
                    if j == 8:
                        b.waits("gpsimd", [xr_free, yr_free, ya_done[g]])
                        d_res = []
                        for tt4 in range(4):
                            d_res.append(GQ.dma(lambda e, g=g, tt4=tt4: e.dma_start(
                                xr_sb[:, tt4 * 2048:(tt4 + 1) * 2048],
                                x_d[g * 512 + tt4 * 128:g * 512 + (tt4 + 1) * 128, :])))
                            d_res.append(GQ.dma(lambda e, g=g, tt4=tt4: e.dma_start(
                                yr_sb[:, tt4 * 2048:(tt4 + 1) * 2048],
                                ya_scr[g * 512 + tt4 * 128:g * 512 + (tt4 + 1) * 128, :])))
                    pg, pu = PS[(j % 2) * 2], PS[(j % 2) * 2 + 1]
                    b.waits("tensor", [d_g[s_g], d_xg,
                                       ps_free[(j % 2) * 2]])
                    for ec in range(16):
                        b.op("tensor", lambda e, gs=gs, ec=ec, s_g=s_g, pg=pg: e.matmul(
                            pg[:, :], wg_sb[s_g][:, ec * 128:(ec + 1) * 128],
                            xnt_g[gs][:, ec * 512:(ec + 1) * 512],
                            start=(ec == 0), stop=(ec == 15)),
                            T if ec == 15 else None)
                    t_g = b.cnt[T]
                    b.waits("tensor", [d_g[s_u], ps_free[(j % 2) * 2 + 1]])
                    for ec in range(16):
                        b.op("tensor", lambda e, gs=gs, ec=ec, s_u=s_u, pu=pu: e.matmul(
                            pu[:, :], wg_sb[s_u][:, ec * 128:(ec + 1) * 128],
                            xnt_g[gs][:, ec * 512:(ec + 1) * 512],
                            start=(ec == 0), stop=(ec == 15)),
                            T if ec == 15 else None)
                    t_u = b.cnt[T]
                    wg_free[s_g] = (T, t_g)
                    wg_free[s_u] = (T, t_u)
                    ssl = j % 2
                    b.waits("scalar", [(T, t_g), sil_free[ssl]])
                    s_si = b.op("scalar", lambda e, pg=pg, ssl=ssl: e.activation(
                        sil_sb[ssl][:, :], pg[:, :], AF.Silu), Sc)
                    ps_free[(j % 2) * 2] = (Sc, s_si)
                    b.waits("vector", [(Sc, s_si), (T, t_u),
                                       (T, h_free_T) if h_free_T else None])
                    v_h = b.op("vector", lambda e, j=j, ssl=ssl, pu=pu: e.tensor_tensor(
                        h_sb[:, j * 512:(j + 1) * 512], sil_sb[ssl][:, :], pu[:, :],
                        ALU.mult), V)
                    sil_free[ssl] = (V, v_h)
                    ps_free[(j % 2) * 2 + 1] = (V, v_h)
                h_ready = b.cnt[V]
                xnt_g_free[gs] = (T, b.cnt[T])

                for eg in range(4):
                    if g == 0 and eg == 0:
                        load_down(0, 0, 0)
                    banks = [PS[2 + k] for k in range(4)]
                    b.waits("tensor", [(V, h_ready)]
                            + [ps_free[2 + k] for k in range(4)])
                    for ic in range(64):
                        isl = (ic // 4) % 2
                        if ic % 4 == 0:
                            nic = ic + 4
                            neg, ng2 = eg, g
                            if nic >= 64:
                                nic = 0
                                neg = eg + 1
                                if neg >= 4:
                                    neg, ng2 = 0, g + 1
                            if ng2 < 4:
                                load_down(neg, nic, 1 - isl)
                        b.waits("tensor", [d_d[isl]])
                        for tt4 in range(4):
                            b.op("tensor", lambda e, ic=ic, tt4=tt4, isl=isl,
                                 banks=banks: e.matmul(
                                     banks[tt4][:, :],
                                     h_sb[:, ic * 512 + tt4 * 128:ic * 512 + (tt4 + 1) * 128],
                                     wd_sb[isl][:, (ic % 4) * 512:(ic % 4 + 1) * 512],
                                     start=(ic == 0), stop=(ic == 63)),
                                 T if (tt4 == 3 and ic % 4 == 3) else None)
                        if ic % 4 == 3:
                            wd_free[isl] = (T, b.cnt[T])
                    t_dn = b.cnt[T]
                    b.waits("scalar", [(T, t_dn), ydn_free])
                    for tt4 in range(4):
                        b.op("scalar", lambda e, tt4=tt4, banks=banks: e.copy(
                            ydn_sb[:, tt4 * 512:(tt4 + 1) * 512],
                            banks[tt4][:, :]), Sc if tt4 == 3 else None)
                    s_dn = b.cnt[Sc]
                    for k in range(4):
                        ps_free[2 + k] = (Sc, s_dn)
                    for tt4 in range(4):
                        osl = tt4 % 2
                        b.waits("vector", d_res + [(Sc, s_dn),
                                           out_free[osl]])
                        v_o1 = b.op("vector", lambda e, tt4=tt4, eg=eg, osl=osl:
                                    e.tensor_tensor(
                                        out_sb[osl][:, :],
                                        xr_sb[:, tt4 * 2048 + eg * 512:tt4 * 2048 + (eg + 1) * 512],
                                        yr_sb[:, tt4 * 2048 + eg * 512:tt4 * 2048 + (eg + 1) * 512],
                                        ALU.add), V)
                        b.wait("vector", V, v_o1)
                        v_ob = b.op("vector", lambda e, osl=osl, tt4=tt4:
                                    e.tensor_tensor(
                                        out_sb[osl][:, :], out_sb[osl][:, :],
                                        ydn_sb[:, tt4 * 512:(tt4 + 1) * 512],
                                        ALU.add), V)
                        b.wait("sync", V, v_ob)
                        out_dma = SQ.dma(lambda e, g=g, tt4=tt4, eg=eg, osl=osl:
                                       e.dma_start(
                                           out_d[g * 512 + tt4 * 128:g * 512 + (tt4 + 1) * 128,
                                                 eg * 512:(eg + 1) * 512],
                                           out_sb[osl][:, :]))
                        out_free[osl] = out_dma
                    v_cmb = b.cnt[V]
                    ydn_free = (V, v_cmb)
                    if eg == 3:
                        xr_free = (V, v_cmb)
                        yr_free = (V, v_cmb)
                h_free_T = b.cnt[T]

            b.waits("sync", [out_dma])
            SQ.drain()
            GQ.drain()
            b.flush("FFN")

    return nc


def _host_inputs(inp):
    x = np.ascontiguousarray(np.asarray(inp["L0_attn_x"], dtype=np.float32))
    w_q_a = np.asarray(inp["L0_attn_w_q_a"], dtype=np.float32)
    lnq = np.asarray(inp["L0_attn_q_a_ln_w"], dtype=np.float32)
    w_q_b = np.asarray(inp["L0_attn_w_q_b"], dtype=np.float32)
    w_kv_a = np.asarray(inp["L0_attn_w_kv_a"], dtype=np.float32)
    lnkv = np.asarray(inp["L0_attn_kv_a_ln_w"], dtype=np.float32)
    w_kv_b = np.asarray(inp["L0_attn_w_kv_b"], dtype=np.float32)
    w_o = np.asarray(inp["L0_attn_w_o"], dtype=np.float32)
    w_gup = np.asarray(inp["L0_ffn_w_gate_up"], dtype=np.float32)
    w_down = np.asarray(inp["L0_ffn_w_down"], dtype=np.float32)

    com = {}
    com["wqa"] = np.ascontiguousarray(w_q_a.reshape(16, 128, QLR))
    com["wkva"] = np.ascontiguousarray(w_kv_a.reshape(16, 128, R + DR))
    qb = w_q_b.reshape(QLR, H, DN + DR)
    com["wqbpe"] = np.ascontiguousarray(
        qb[:, :, DN:].reshape(QLR, H * DR).reshape(12, 128, H * DR))
    com["wqbnope"] = np.ascontiguousarray(
        qb[:, :, :DN].reshape(12, 128, H, DN).transpose(2, 0, 1, 3))
    kvb = w_kv_b.reshape(R, H, DN + DV)
    com["wukt"] = np.ascontiguousarray(kvb[:, :, :DN].transpose(1, 2, 0))
    com["wuv"] = np.ascontiguousarray(
        kvb[:, :, DN:].reshape(4, 128, H, DV).transpose(2, 0, 1, 3))
    com["wo"] = np.ascontiguousarray(
        w_o.reshape(H, 128, 4, 512).transpose(2, 0, 1, 3))
    com["wgup"] = np.ascontiguousarray(
        w_gup.reshape(128, 128, 16, 128).transpose(0, 2, 3, 1))
    com["wdown"] = np.ascontiguousarray(
        w_down.reshape(4, 512, 64, 128).transpose(0, 2, 3, 1))
    com["lnq"] = np.ascontiguousarray(lnq.reshape(1, QLR))
    com["lnkv"] = np.ascontiguousarray(lnkv.reshape(1, R))

    t = np.arange(S, dtype=np.float64)
    inv_freq = 1.0 / (10000.0 ** (np.arange(0, DR, 2, dtype=np.float64) / DR))
    ang = t[:, None] * inv_freq[None, :]
    cos32 = np.cos(ang).astype(np.float32)
    sin32 = np.sin(ang).astype(np.float32)
    cos64 = np.concatenate([cos32, cos32], axis=1)
    sinp = np.concatenate([-sin32, sin32], axis=1)
    com["cosx2"] = np.ascontiguousarray(
        np.concatenate([cos64, cos64], axis=1).reshape(16, 128, 128))
    com["sinx2"] = np.ascontiguousarray(
        np.concatenate([sinp, sinp], axis=1).reshape(16, 128, 128))
    com["ident"] = np.eye(128, dtype=np.float32)
    m = np.zeros((128, 128), dtype=np.float32)
    m[np.triu_indices(128, 1)] = NEG
    com["maskneg"] = m
    com["ones1"] = np.ones((1, 128), dtype=np.float32)

    maps = []
    for c in range(8):
        d = dict(com)
        d["x"] = np.ascontiguousarray(x[c])
        maps.append(d)
    return maps


def kernel(**inputs):
    maps = _host_inputs(inputs)
    nc = build_nc()
    res = run_bass_kernel_spmd(nc, maps, core_ids=list(range(8)))
    return np.stack([np.asarray(r["out"], dtype=np.float32) for r in res.results], axis=0)


# version 14
# speedup vs baseline: 1.0085x; 1.0085x over previous
import sys

import numpy as np

try:
    import concourse.bass as bass
except ImportError:
    sys.path.insert(0, "/opt/trn_rl_repo")
    import concourse.bass as bass
import concourse.mybir as mybir
from concourse.bass_utils import run_bass_kernel_spmd
from contextlib import ExitStack

F32 = mybir.dt.float32
BF16 = mybir.dt.bfloat16
AF = mybir.ActivationFunctionType
ALU = mybir.AluOpType
AX = mybir.AxisListType

S = 2048
E = 2048
H = 16
DN, DR, DV, R, QLR = 128, 64, 128, 512, 1536
SM = float((DN + DR) ** -0.5)
EPS_LN = 1e-5
EPS_RMS = 1e-6
NT = S // 128
NB = S // 512
NEG = -1.0e30

DEBUG = False
PHASES = 4
P1STAGE = 9


class Bld:
    ENGS = ("tensor", "vector", "scalar", "gpsimd", "sync")

    def __init__(self, nc):
        self.nc = nc
        self.q = {e: [] for e in self.ENGS}
        self.sems = {}
        self.cnt = {}
        self.hw = {e: {} for e in self.ENGS}

    def new_sem(self, st, name):
        self.sems[name] = st.enter_context(self.nc.semaphore(name))
        self.cnt[name] = 0
        return name

    def op(self, eng, fn, inc=None, amt=1):
        if inc is None:
            self.q[eng].append(lambda e, fn=fn: fn(e))
            return None
        h = self.sems[inc]
        self.q[eng].append(lambda e, fn=fn, h=h, a=amt: fn(e).then_inc(h, a))
        self.cnt[inc] += amt
        return self.cnt[inc]

    def wait(self, eng, sem, val):
        if val is None or val <= 0:
            return
        assert val <= self.cnt[sem], f"wait-on-future {sem} {val}>{self.cnt[sem]}"
        if self.hw[eng].get(sem, 0) >= val:
            return
        self.hw[eng][sem] = val
        h = self.sems[sem]
        self.q[eng].append(lambda e, h=h, v=val: e.wait_ge(h, v))

    def waits(self, eng, deps):
        best = {}
        for p in deps:
            if not p:
                continue
            if isinstance(p, list):
                items = p
            else:
                items = [p]
            for s, v in items:
                if v and (s not in best or v > best[s]):
                    best[s] = v
        for s, v in best.items():
            self.wait(eng, s, v)

    def flush(self, name):
        with self.nc.Block(name) as blk:
            for eng in self.ENGS:
                if self.q[eng]:
                    getattr(blk, eng)(lambda e, lst=list(self.q[eng]): [f(e) for f in lst])
        self.q = {e: [] for e in self.ENGS}



class DmaQ:

    def __init__(self, b, st, eng, name, k=8):
        self.b = b
        self.eng = eng
        self.k = k
        self.names = [b.new_sem(st, f"{name}{i}") for i in range(k)]
        self.n = 0

    def dma(self, fn):
        b = self.b
        i = self.n
        self.n += 1
        slot = i % self.k
        sem = self.names[slot]
        uses = i // self.k
        if uses > 0:
            b.wait(self.eng, sem, 16 * uses)
        cnt = b.op(self.eng, fn, sem, 16)
        return (sem, cnt)

    def drain(self):
        for sem in self.names:
            self.b.wait(self.eng, sem, self.b.cnt[sem])


def _dma3(dst, src3, c):
    return lambda e: e.dma_start(
        dst[:, :].rearrange("p (a c) -> p a c", c=c),
        src3.rearrange("a b c -> b a c"))


def _dma3_out(dst3, src, c):
    return lambda e: e.dma_start(
        dst3.rearrange("a b c -> b a c"),
        src[:, :].rearrange("p (a c) -> p a c", c=c))


def build_nc():
    nc = bass.Bass("TRN2", target_bir_lowering=False)

    x_d = nc.dram_tensor("x", [S, E], F32, kind="ExternalInput").ap()
    wqa = nc.dram_tensor("wqa", [16, 128, QLR], F32, kind="ExternalInput").ap()
    wkva = nc.dram_tensor("wkva", [16, 128, R + DR], F32, kind="ExternalInput").ap()
    wqbpe = nc.dram_tensor("wqbpe", [12, 128, H * DR], F32, kind="ExternalInput").ap()
    wqbnope = nc.dram_tensor("wqbnope", [H, 12, 128, DN], F32, kind="ExternalInput").ap()
    wukt = nc.dram_tensor("wukt", [H, 128, R], F32, kind="ExternalInput").ap()
    wuv = nc.dram_tensor("wuv", [H, 4, 128, DV], F32, kind="ExternalInput").ap()
    wo = nc.dram_tensor("wo", [4, H, 128, 512], F32, kind="ExternalInput").ap()
    wgup = nc.dram_tensor("wgup", [128, 16, 128, 128], F32, kind="ExternalInput").ap()
    wdown = nc.dram_tensor("wdown", [4, 64, 128, 512], F32, kind="ExternalInput").ap()
    lnq_d = nc.dram_tensor("lnq", [1, QLR], F32, kind="ExternalInput").ap()
    lnkv_d = nc.dram_tensor("lnkv", [1, R], F32, kind="ExternalInput").ap()
    cosx2_d = nc.dram_tensor("cosx2", [16, 128, 128], F32, kind="ExternalInput").ap()
    sinx2_d = nc.dram_tensor("sinx2", [16, 128, 128], F32, kind="ExternalInput").ap()
    ident_d = nc.dram_tensor("ident", [128, 128], F32, kind="ExternalInput").ap()
    mask_d = nc.dram_tensor("maskneg", [128, 128], F32, kind="ExternalInput").ap()
    ones_d = nc.dram_tensor("ones1", [1, 128], F32, kind="ExternalInput").ap()

    out_d = nc.dram_tensor("out", [S, E], F32, kind="ExternalOutput").ap()
    scratch_kind = "ExternalOutput" if DEBUG else "Internal"
    xnt_scr = nc.dram_tensor("xnt_scr", [16, 128, S], BF16, kind=scratch_kind).ap()
    qat_scr = nc.dram_tensor("qat_scr", [12, 128, S], BF16, kind=scratch_kind).ap()
    ya_scr = nc.dram_tensor("ya_scr", [S, E], F32, kind=scratch_kind).ap()
    wgup_bf = nc.dram_tensor("wgup_bf", [128, 16, 128, 128], BF16, kind="Internal").ap()
    wdown_bf = nc.dram_tensor("wdown_bf", [4, 64, 128, 512], BF16, kind="Internal").ap()

    b = Bld(nc)

    with ExitStack() as outer:
        sb = lambda n, sh, dt: outer.enter_context(nc.sbuf_tensor(n, sh, dt))
        SQ = DmaQ(b, outer, "sync", "Ds", 8)
        GQ = DmaQ(b, outer, "gpsimd", "Dg", 8)
        T = b.new_sem(outer, "T")
        V = b.new_sem(outer, "V")
        Sc = b.new_sem(outer, "Sc")

        PS = [outer.enter_context(nc.psum_tensor(f"ps{i}", [128, 512], F32)) for i in range(6)]
        PT = [outer.enter_context(nc.psum_tensor(f"pt{i}", [128, 1024], BF16)) for i in range(2)]
        ps_free = [None] * 6
        pt_free = [None] * 2

        ident = sb("ident_sb", [128, 128], BF16)
        maskneg = sb("maskneg_sb", [128, 128], F32)
        attres = ExitStack()
        asb = lambda n, sh, dt: attres.enter_context(nc.sbuf_tensor(n, sh, dt))
        kT = [asb(f"kT{i}", [128, S], BF16) for i in range(4)]
        kTpe_lo = asb("kTpe_lo", [128, S], BF16)
        kTpe_hi = asb("kTpe_hi", [128, S], BF16)
        ckv_nat = asb("ckv_nat", [128, 16 * 512], BF16)
        qpeT = [asb(f"qpeT{i}", [128, S], BF16) for i in range(8)]

        prep = ExitStack()
        psb0 = lambda n, sh, dt: prep.enter_context(nc.sbuf_tensor(n, sh, dt))
        cosx2 = psb0("cosx2_sb", [128, 16 * 128], F32)
        sinx2 = psb0("sinx2_sb", [128, 16 * 128], F32)
        lnq_rep = psb0("lnq_rep", [128, QLR], F32)
        lnkv_rep = psb0("lnkv_rep", [128, R], F32)
        lnq_row = psb0("lnq_row", [1, QLR], F32)
        lnkv_row = psb0("lnkv_row", [1, R], F32)
        ones_row = psb0("ones_row", [1, 128], F32)

        with ExitStack() as ph:
            psb = lambda n, sh, dt: ph.enter_context(nc.sbuf_tensor(n, sh, dt))
            x_t = [psb(f"x_t{i}", [128, E], F32) for i in range(2)]
            xn_t = [psb(f"xn_t{i}", [128, E], BF16) for i in range(2)]
            xnt_t = [psb(f"xnt_t{i}", [128, E], BF16) for i in range(2)]
            wkva_sb = psb("wkva_sb", [128, 16 * (R + DR)], BF16)
            stat6 = psb("stat6", [128, 24], F32)
            stat2 = psb("stat2", [128, 2], F32)
            inv_s = psb("inv_s", [128, 1], F32)
            nbias = psb("nbias", [128, 1], F32)
            rs_sb = psb("rs_sb", [128, 1], F32)
            vk_sb = psb("vk_sb", [128, 1], F32)
            invk = psb("invk", [128, 1], F32)
            ckv_f = psb("ckv_f", [128, R], F32)
            km1 = psb("km1", [128, 64], F32)
            km2 = psb("km2", [128, 64], F32)
            krop_lo = psb("krop_lo", [128, 128], BF16)
            krop_hi = psb("krop_hi", [128, 128], BF16)

            d_mask = SQ.dma(lambda e: e.dma_start(maskneg[:, :], mask_d[:, :]))
            d_lnq = SQ.dma(lambda e: e.dma_start(lnq_row[:, :], lnq_d[:, :]))
            d_lnkv = SQ.dma(lambda e: e.dma_start(lnkv_row[:, :], lnkv_d[:, :]))
            d_ones = SQ.dma(lambda e: e.dma_start(ones_row[:, :], ones_d[:, :]))
            d_cosA = SQ.dma(_dma3(cosx2, cosx2_d[:, :, :], 128))
            d_cosB = SQ.dma(_dma3(sinx2, sinx2_d[:, :, :], 128))
            d_cos = [d_cosA, d_cosB]
            d_ident = GQ.dma(lambda e: e.dma_start(ident[:, :], ident_d[:, :]))
            d_wkva = GQ.dma(_dma3(wkva_sb, wkva[:, :, :], 576))

            b.waits("tensor", [d_lnq, d_lnkv, d_ones])
            for j in range(3):
                b.op("tensor", lambda e, j=j: e.matmul(
                    PS[j][:, :], ones_row[:, :], lnq_row[:, j * 512:(j + 1) * 512],
                    start=True, stop=True), T if j == 2 else None)
            b.op("tensor", lambda e: e.matmul(
                PS[3][:, :], ones_row[:, :], lnkv_row[:, :], start=True, stop=True), T)
            t_rep = b.cnt[T]
            b.wait("vector", T, t_rep)
            for j in range(3):
                b.op("vector", lambda e, j=j: e.tensor_copy(
                    lnq_rep[:, j * 512:(j + 1) * 512], PS[j][:, :]))
            v_rep = b.op("vector", lambda e: e.tensor_copy(lnkv_rep[:, :], PS[3][:, :]), V)
            for j in range(4):
                ps_free[j] = (V, v_rep)

            b.op("vector", lambda e: e.memset(krop_lo[:, 64:128], 0.0))
            v_ms = b.op("vector", lambda e: e.memset(krop_hi[:, 0:64], 0.0), V)
            xnt_dma = [None] * NT
            x_free = [None, None]
            xn_free = [None, None]
            xnt_free = [None, None]
            ckv_ready = 0
            kT_ready = 0

            for tt in range(NT):
                sl = tt % 2
                b.waits("sync", [x_free[sl]])
                d_x = SQ.dma(lambda e, tt=tt, sl=sl: e.dma_start(
                    x_t[sl][:, :], x_d[tt * 128:(tt + 1) * 128, :]))
                b.waits("vector", [d_x])
                for sg in range(4):
                    b.op("vector", lambda e, sl=sl, sg=sg: e.bn_stats(
                        stat6[:, sg * 6:(sg + 1) * 6],
                        x_t[sl][:, sg * 512:(sg + 1) * 512]),
                        V if sg == 3 else None)
                v1 = b.cnt[V]
                b.wait("vector", V, v1)
                v2 = b.op("vector", lambda e: e.bn_aggr(stat2[:, :], stat6[:, :]), V)
                b.wait("vector", V, v2)
                v2b = b.op("vector", lambda e: e.tensor_scalar_add(
                    stat2[:, 1:2], stat2[:, 1:2], EPS_LN), V)
                b.wait("scalar", V, v2b)
                s1 = b.op("scalar", lambda e: e.activation(
                    inv_s[:, :], stat2[:, 1:2], AF.Ln), Sc)
                b.wait("scalar", Sc, s1)
                s2 = b.op("scalar", lambda e: e.activation(
                    inv_s[:, :], inv_s[:, :], AF.Exp, scale=-0.5), Sc)
                b.wait("vector", Sc, s2)
                v3 = b.op("vector", lambda e: e.scalar_tensor_tensor(
                    nbias[:, :], stat2[:, 0:1], -1.0, inv_s[:, :], ALU.mult, ALU.mult), V)
                b.waits("vector", [(V, v3), xn_free[sl]])
                s_xn = b.op("vector", lambda e, sl=sl: e.tensor_scalar(
                    xn_t[sl][:, :], x_t[sl][:, :], inv_s[:, :], nbias[:, :],
                    ALU.mult, ALU.add), V)
                x_free[sl] = [(V, s_xn)]
                if P1STAGE <= 1:
                    continue
                b.waits("tensor", [(V, s_xn), d_ident])
                b.waits("vector", [xnt_free[sl]])
                xnt_free[sl] = []
                for half in range(2):
                    pb = half
                    b.waits("tensor", [pt_free[pb]])
                    for j in range(8):
                        ec = half * 8 + j
                        b.op("tensor", lambda e, sl=sl, pb=pb, j=j, ec=ec: e.matmul(
                            PT[pb][:, j * 128:(j + 1) * 128],
                            xn_t[sl][:, ec * 128:(ec + 1) * 128], ident[:, :],
                            is_transpose=True, start=(j == 0), stop=(j == 7)),
                            T if j == 7 else None)
                    t_tr = b.cnt[T]
                    b.wait("vector", T, t_tr)
                    v_ev = b.op("vector", lambda e, sl=sl, half=half, pb=pb: e.tensor_copy(
                        xnt_t[sl][:, half * 1024:(half + 1) * 1024], PT[pb][:, :]), V)
                    pt_free[pb] = (V, v_ev)
                v_xnt = b.cnt[V]
                xn_free[sl] = (V, v_xnt)
                b.wait("sync", V, v_xnt)
                xnt_dma[tt] = SQ.dma((lambda sl, tt: _dma3_out(xnt_scr[:, :, tt * 128:(tt + 1) * 128], xnt_t[sl], 128))(sl, tt))
                if P1STAGE <= 2:
                    continue
                b.waits("tensor", [d_wkva, ps_free[4], ps_free[5], (V, v_xnt)])
                for ec in range(16):
                    b.op("tensor", lambda e, sl=sl, ec=ec: e.matmul(
                        PS[4][:, :], xnt_t[sl][:, ec * 128:(ec + 1) * 128],
                        wkva_sb[:, ec * 576:ec * 576 + 512],
                        start=(ec == 0), stop=(ec == 15)),
                        T if ec == 15 else None)
                t_ckv = b.cnt[T]
                for ec in range(16):
                    b.op("tensor", lambda e, sl=sl, ec=ec: e.matmul(
                        PS[5][:, 0:64], xnt_t[sl][:, ec * 128:(ec + 1) * 128],
                        wkva_sb[:, ec * 576 + 512:ec * 576 + 576],
                        start=(ec == 0), stop=(ec == 15)),
                        T if ec == 15 else None)
                t_kpe = b.cnt[T]
                xnt_free[sl] = [(T, t_kpe), xnt_dma[tt]]
                if P1STAGE <= 3:
                    continue
                b.wait("scalar", T, t_ckv)
                s3 = b.op("scalar", lambda e: e.square(ckv_f[:, :], PS[4][:, :]), Sc)
                b.wait("vector", Sc, s3)
                v4 = b.op("vector", lambda e: e.reduce_sum(rs_sb[:, :], ckv_f[:, :], AX.X), V)
                b.wait("vector", V, v4)
                v5 = b.op("vector", lambda e: e.tensor_scalar(
                    vk_sb[:, :], rs_sb[:, :], 1.0 / R, EPS_RMS, ALU.mult, ALU.add), V)
                b.wait("scalar", V, v5)
                s4 = b.op("scalar", lambda e: e.activation(invk[:, :], vk_sb[:, :], AF.Ln), Sc)
                b.wait("scalar", Sc, s4)
                s5 = b.op("scalar", lambda e: e.activation(
                    invk[:, :], invk[:, :], AF.Exp, scale=-0.5), Sc)
                b.wait("scalar", Sc, s5)
                s6 = b.op("scalar", lambda e: e.activation(
                    ckv_f[:, :], PS[4][:, :], AF.Copy, scale=invk[:, :]), Sc)
                ps_free[4] = (Sc, s6)
                b.wait("vector", Sc, s6)
                v6 = b.op("vector", lambda e, tt=tt: e.tensor_tensor(
                    ckv_nat[:, tt * 512:(tt + 1) * 512], ckv_f[:, :], lnkv_rep[:, :],
                    ALU.mult), V)
                ckv_ready = v6
                if P1STAGE <= 4:
                    continue
                b.waits("vector", [(T, t_kpe)] + d_cos)
                b.op("vector", lambda e, tt=tt: e.tensor_tensor(
                    km1[:, :], PS[5][:, 0:64], cosx2[:, tt * 128:tt * 128 + 64], ALU.mult))
                b.op("vector", lambda e, tt=tt: e.tensor_tensor(
                    km2[:, 0:32], PS[5][:, 32:64], sinx2[:, tt * 128:tt * 128 + 32], ALU.mult))
                v7 = b.op("vector", lambda e, tt=tt: e.tensor_tensor(
                    km2[:, 32:64], PS[5][:, 0:32], sinx2[:, tt * 128 + 32:tt * 128 + 64],
                    ALU.mult), V)
                b.wait("vector", V, v7)
                b.op("vector", lambda e: e.tensor_tensor(
                    krop_lo[:, 0:64], km1[:, :], km2[:, :], ALU.add))
                v8 = b.op("vector", lambda e: e.tensor_tensor(
                    krop_hi[:, 64:128], km1[:, :], km2[:, :], ALU.add), V)
                ps_free[5] = (V, v8)
                if P1STAGE <= 5:
                    continue
                b.waits("tensor", [(V, v6), (V, v8), (V, v_ms), pt_free[0]])
                for rc in range(4):
                    b.op("tensor", lambda e, tt=tt, rc=rc: e.matmul(
                        PT[0][:, rc * 128:(rc + 1) * 128],
                        ckv_nat[:, tt * 512 + rc * 128:tt * 512 + (rc + 1) * 128],
                        ident[:, :], is_transpose=True, start=(rc == 0), stop=False))
                b.op("tensor", lambda e: e.matmul(
                    PT[0][:, 512:640], krop_lo[:, :], ident[:, :],
                    is_transpose=True, start=False, stop=False))
                t_k5 = b.op("tensor", lambda e: e.matmul(
                    PT[0][:, 640:768], krop_hi[:, :], ident[:, :],
                    is_transpose=True, start=False, stop=True), T)
                b.wait("vector", T, t_k5)
                for rc in range(4):
                    b.op("vector", lambda e, tt=tt, rc=rc: e.tensor_copy(
                        kT[rc][:, tt * 128:(tt + 1) * 128], PT[0][:, rc * 128:(rc + 1) * 128]))
                b.op("vector", lambda e, tt=tt: e.tensor_copy(
                    kTpe_lo[:, tt * 128:(tt + 1) * 128], PT[0][:, 512:640]))
                v9 = b.op("vector", lambda e, tt=tt: e.tensor_copy(
                    kTpe_hi[:, tt * 128:(tt + 1) * 128], PT[0][:, 640:768]), V)
                pt_free[0] = (V, v9)
                kT_ready = v9

            if PHASES <= 1:
                SQ.drain()
                GQ.drain()
            SQ.drain()
            GQ.drain()
            b.flush("P1")
        if PHASES <= 1:
            prep.close()
            attres.close()
            return nc

        with ExitStack() as ph:
            psb = lambda n, sh, dt: ph.enter_context(nc.sbuf_tensor(n, sh, dt))
            xnt_t = [psb(f"p2xnt{i}", [128, E], BF16) for i in range(2)]
            wqa_sb = psb("wqa_sb", [128, 16 * QLR], BF16)
            wqbpe_sb = psb("wqbpe_sb", [128, 12 * 1024], BF16)
            qa_f = psb("qa_f", [128, QLR], F32)
            qa_bf = [psb(f"qa_bf{i}", [128, QLR], BF16) for i in range(2)]
            qat_t = [psb(f"qat_t{i}", [128, 12 * 128], BF16) for i in range(2)]
            qpe_bf = psb("qpe_bf", [128, 1024], BF16)
            rs2 = psb("rs2", [128, 1], F32)
            vq2 = psb("vq2", [128, 1], F32)
            invq = psb("invq", [128, 1], F32)
            qm1 = psb("qm1", [128, 256], F32)
            qm2 = psb("qm2", [128, 256], F32)

            d_wqa = GQ.dma(_dma3(wqa_sb, wqa[:, :, :], 1536))
            d_wpe = GQ.dma(_dma3(wqbpe_sb, wqbpe[:, :, :], 1024))
            xnt_free2 = [None, None]
            qat_free = [None, None]
            qat_dma = [None] * NT
            qpeT_ready = 0

            d_xn_t = [None] * (NT + 1)
            t_qa_t = [None] * NT
            s4_t = [None] * NT
            v3_t = [None] * NT
            v4_t = [None] * NT
            t_pe_t = [None] * NT
            rope_v = [None] * NT
            t_tqa = [None] * NT
            t_tqpe = [None] * NT

            def stage_load(i):
                if i >= NT:
                    return
                b.waits("sync", [xnt_free2[i % 2], xnt_dma[i]])
                d_xn_t[i] = SQ.dma((lambda i: _dma3(
                    xnt_t[i % 2], xnt_scr[:, :, i * 128:(i + 1) * 128], 128))(i))

            def stage_qa(i):
                sl = i % 2
                b.waits("tensor", [d_wqa, d_xn_t[i],
                                   ps_free[0], ps_free[1], ps_free[2]])
                for lg in range(3):
                    for ec in range(16):
                        b.op("tensor", lambda e, sl=sl, lg=lg, ec=ec: e.matmul(
                            PS[lg][:, :], xnt_t[sl][:, ec * 128:(ec + 1) * 128],
                            wqa_sb[:, ec * QLR + lg * 512:ec * QLR + (lg + 1) * 512],
                            start=(ec == 0), stop=(ec == 15)),
                            T if ec == 15 else None)
                t_qa_t[i] = b.cnt[T]
                xnt_free2[sl] = (T, t_qa_t[i])

            def stage_norm(i):
                b.waits("scalar", [(T, t_qa_t[i]),
                                   (V, v3_t[i - 1]) if i > 0 else None])
                for lg in range(3):
                    b.op("scalar", lambda e, lg=lg: e.square(
                        qa_f[:, lg * 512:(lg + 1) * 512], PS[lg][:, :]),
                        Sc if lg == 2 else None)
                s1 = b.cnt[Sc]
                b.wait("vector", Sc, s1)
                v1 = b.op("vector", lambda e: e.reduce_sum(rs2[:, :], qa_f[:, :], AX.X), V)
                b.wait("vector", V, v1)
                v2 = b.op("vector", lambda e: e.tensor_scalar(
                    vq2[:, :], rs2[:, :], 1.0 / QLR, EPS_RMS, ALU.mult, ALU.add), V)
                b.wait("scalar", V, v2)
                s2 = b.op("scalar", lambda e: e.activation(invq[:, :], vq2[:, :], AF.Ln), Sc)
                b.wait("scalar", Sc, s2)
                s3 = b.op("scalar", lambda e: e.activation(
                    invq[:, :], invq[:, :], AF.Exp, scale=-0.5), Sc)
                b.wait("scalar", Sc, s3)
                for lg in range(3):
                    b.op("scalar", lambda e, lg=lg: e.activation(
                        qa_f[:, lg * 512:(lg + 1) * 512], PS[lg][:, :], AF.Copy,
                        scale=invq[:, :]), Sc if lg == 2 else None)
                s4_t[i] = b.cnt[Sc]
                for lg in range(3):
                    ps_free[lg] = (Sc, s4_t[i])
                b.waits("vector", [(Sc, s4_t[i]),
                                   (T, t_tqa[i - 2]) if i >= 2 else None])
                v3_t[i] = b.op("vector", lambda e, i=i: e.tensor_tensor(
                    qa_bf[i % 2][:, :], qa_f[:, :], lnq_rep[:, :], ALU.mult), V)

            def stage_tqa(i):
                sl = i % 2
                b.waits("tensor", [(V, v3_t[i]), pt_free[0], pt_free[1],
                                   qat_free[sl]])
                for lc in range(12):
                    pb = lc // 8
                    b.op("tensor", lambda e, lc=lc, pb=pb, i=i: e.matmul(
                        PT[pb][:, (lc % 8) * 128:((lc % 8) + 1) * 128],
                        qa_bf[i % 2][:, lc * 128:(lc + 1) * 128], ident[:, :],
                        is_transpose=True, start=(lc % 8 == 0),
                        stop=(lc in (7, 11))),
                        T if lc in (7, 11) else None)
                t_tqa[i] = b.cnt[T]
                b.wait("vector", T, t_tqa[i])
                b.op("vector", lambda e, sl=sl: e.tensor_copy(
                    qat_t[sl][:, 0:1024], PT[0][:, :]))
                v4_t[i] = b.op("vector", lambda e, sl=sl: e.tensor_copy(
                    qat_t[sl][:, 1024:1536], PT[1][:, 0:512]), V)
                pt_free[0] = (V, v4_t[i])
                pt_free[1] = (V, v4_t[i])

            def stage_spill(i):
                sl = i % 2
                b.wait("sync", V, v4_t[i])
                qat_dma[i] = SQ.dma((lambda sl, i: _dma3_out(
                    qat_scr[:, :, i * 128:(i + 1) * 128], qat_t[sl], 128))(sl, i))

            def stage_qpe(i):
                sl = i % 2
                b.waits("tensor", [d_wpe, (V, v4_t[i]), ps_free[3], ps_free[4]])
                for ng in range(2):
                    for lc in range(12):
                        b.op("tensor", lambda e, sl=sl, ng=ng, lc=lc: e.matmul(
                            PS[3 + ng][:, :], qat_t[sl][:, lc * 128:(lc + 1) * 128],
                            wqbpe_sb[:, lc * 1024 + ng * 512:lc * 1024 + (ng + 1) * 512],
                            start=(lc == 0), stop=(lc == 11)),
                            T if lc == 11 else None)
                t_pe_t[i] = b.cnt[T]
                qat_free[sl] = [(T, t_pe_t[i]), qat_dma[i]]

            def stage_rope(i):
                b.waits("vector", [(T, t_pe_t[i]),
                                   (T, t_tqpe[i - 1]) if i >= 1 else None])
                for hp in range(8):
                    bank = PS[3 + hp // 4]
                    o = (hp % 4) * 128
                    q0 = (hp % 2) * 128
                    b.op("vector", lambda e, bank=bank, o=o, tt=i, q0=q0: e.tensor_tensor(
                        qm1[:, q0:q0 + 128], bank[:, o:o + 128],
                        cosx2[:, tt * 128:(tt + 1) * 128], ALU.mult))
                    b.op("vector", lambda e, bank=bank, o=o, tt=i, q0=q0: e.tensor_tensor(
                        qm2[:, q0:q0 + 32], bank[:, o + 32:o + 64],
                        sinx2[:, tt * 128:tt * 128 + 32], ALU.mult))
                    b.op("vector", lambda e, bank=bank, o=o, tt=i, q0=q0: e.tensor_tensor(
                        qm2[:, q0 + 32:q0 + 64], bank[:, o:o + 32],
                        sinx2[:, tt * 128 + 32:tt * 128 + 64], ALU.mult))
                    b.op("vector", lambda e, bank=bank, o=o, tt=i, q0=q0: e.tensor_tensor(
                        qm2[:, q0 + 64:q0 + 96], bank[:, o + 96:o + 128],
                        sinx2[:, tt * 128 + 64:tt * 128 + 96], ALU.mult))
                    v5 = b.op("vector", lambda e, bank=bank, o=o, tt=i, q0=q0: e.tensor_tensor(
                        qm2[:, q0 + 96:q0 + 128], bank[:, o + 64:o + 96],
                        sinx2[:, tt * 128 + 96:tt * 128 + 128], ALU.mult), V)
                    b.wait("vector", V, v5)
                    b.op("vector", lambda e, hp=hp, q0=q0: e.tensor_tensor(
                        qpe_bf[:, hp * 128:(hp + 1) * 128], qm1[:, q0:q0 + 128],
                        qm2[:, q0:q0 + 128], ALU.add),
                        V if hp == 7 else None)
                rope_v[i] = b.cnt[V]
                ps_free[3] = (V, rope_v[i])
                ps_free[4] = (V, rope_v[i])

            def stage_tqpe(i):
                nonlocal qpeT_ready
                b.waits("tensor", [(V, rope_v[i]), pt_free[0]])
                for hp in range(8):
                    b.op("tensor", lambda e, hp=hp: e.transpose(
                        PT[0][:, hp * 128:(hp + 1) * 128],
                        qpe_bf[:, hp * 128:(hp + 1) * 128], ident[:, :]),
                        T if hp == 7 else None)
                t_tqpe[i] = b.cnt[T]
                b.wait("vector", T, t_tqpe[i])
                for hp in range(8):
                    b.op("vector", lambda e, hp=hp, tt=i: e.tensor_copy(
                        qpeT[hp][:, tt * 128:(tt + 1) * 128],
                        PT[0][:, hp * 128:(hp + 1) * 128]),
                        V if hp == 7 else None)
                qpeT_ready = b.cnt[V]
                pt_free[0] = (V, qpeT_ready)

            stage_load(0)
            for it in range(NT + 2):
                if it < NT:
                    stage_qa(it)
                    stage_load(it + 1)
                    stage_norm(it)
                if it >= 2 and it - 2 < NT:
                    stage_tqpe(it - 2)
                if it >= 1 and it - 1 < NT:
                    stage_tqa(it - 1)
                    stage_spill(it - 1)
                    stage_qpe(it - 1)
                    stage_rope(it - 1)

            if PHASES <= 2:
                SQ.drain()
                GQ.drain()
            SQ.drain()
            GQ.drain()
            b.flush("P2")
        if PHASES <= 2:
            prep.close()
            attres.close()
            return nc

        prep.close()

        with ExitStack() as ph:
            psb = lambda n, sh, dt: ph.enter_context(nc.sbuf_tensor(n, sh, dt))
            qat_B = psb("qat_B", [128, 12 * 512], BF16)
            wukt_sb = [psb(f"wukt{i}", [128, 512], BF16) for i in range(2)]
            wnope_sb = [psb(f"wnope{i}", [128, 12 * 128], BF16) for i in range(2)]
            wuv_sb = [psb(f"wuv{i}", [128, 4 * 128], BF16) for i in range(2)]
            wo_sb = [psb(f"wo{i}", [128, H * 512], BF16) for i in range(2)]
            ptasm = psb("ptasm", [128, 16 * 512], BF16)
            p_nat = [psb(f"p_nat{i}", [128, S], BF16) for i in range(2)]
            qf = [psb(f"qf{i}", [128, 4 * 512], BF16) for i in range(2)]
            qnope_sb = [psb(f"qnope{i}", [128, 512], BF16) for i in range(2)]
            o_sb = psb("o_sb", [128, 4 * 512], BF16)
            t1_sb = psb("t1_sb", [128, H * 512], BF16)
            s_stage = psb("s_stage", [128, S], F32)
            y_sb = [psb(f"y_sb{i}", [128, 512], F32) for i in range(2)]
            mx_part = psb("mx_part", [128, 4], F32)
            mx_row = psb("mx_row", [128, 1], F32)
            e_bias = psb("e_bias", [128, 1], F32)
            sm_sums = psb("sm_sums", [128, 4], F32)
            sm_tot = psb("sm_tot", [128, 1], F32)
            sm_inv = psb("sm_inv", [128, 1], F32)

            b.waits("vector", [d_mask])

            conv_sb = [psb(f"conv{i}", [128, 2048], BF16) for i in range(3)]
            conv_free = [None] * 3
            conv_units = ([("g", oc) for oc in range(128)]
                          + [("d", u) for u in range(64)])
            conv_i = [0]

            def emit_conversions(n):
                for _ in range(n):
                    if conv_i[0] >= len(conv_units):
                        return
                    kind, u = conv_units[conv_i[0]]
                    slot = conv_i[0] % 3
                    conv_i[0] += 1
                    b.waits("gpsimd", [conv_free[slot]])
                    if kind == "g":
                        tok = GQ.dma(_dma3(conv_sb[slot], wgup[u, :, :, :], 128))
                        b.waits("sync", [tok])
                        st = SQ.dma((lambda u, slot: _dma3_out(
                            wgup_bf[u, :, :, :], conv_sb[slot], 128))(u, slot))
                    else:
                        eg, ic0 = u % 4, (u // 4) * 4
                        tok = GQ.dma((lambda eg, ic0, slot: _dma3(
                            conv_sb[slot], wdown[eg, ic0:ic0 + 4, :, :], 512))(eg, ic0, slot))
                        b.waits("sync", [tok])
                        st = SQ.dma((lambda eg, ic0, slot: _dma3_out(
                            wdown_bf[eg, ic0:ic0 + 4, :, :], conv_sb[slot], 512))(eg, ic0, slot))
                    conv_free[slot] = st

            qat_B_free = [None]
            wnope_free = [None, None]
            wuv_free = [None, None]
            wo_free = [None, None]
            ptasm_free = None
            p_nat_free = [None, None]
            qf_free = [None, None]
            qnope_free = [None, None]
            o_sb_free = None
            t1_free = None
            y_free = [None, None]
            ya_dma = None
            ya_done = [None] * NB
            d_nope = [None, None]
            d_uv = [None, None]
            d_wo = [None, None]

            def load_nope(h, slot):
                b.waits("gpsimd", [wnope_free[slot]])
                d_nope[slot] = GQ.dma((lambda h, slot: _dma3(wnope_sb[slot], wqbnope[h, :, :, :], 128))(h, slot))

            wukt_free = [None, None]
            d_ukt = [None, None]

            def load_ukt(h, slot):
                b.waits("gpsimd", [wukt_free[slot]])
                d_ukt[slot] = GQ.dma((lambda h, slot: e_dma_ukt(h, slot))(h, slot))

            def e_dma_ukt(h, slot):
                return lambda e: e.dma_start(wukt_sb[slot][:, :], wukt[h, :, :])

            def load_uv(h, slot):
                b.waits("gpsimd", [wuv_free[slot]])
                d_uv[slot] = GQ.dma((lambda h, slot: _dma3(wuv_sb[slot], wuv[h, :, :, :], 128))(h, slot))

            def load_wo(eg, slot):
                b.waits("gpsimd", [wo_free[slot]])
                d_wo[slot] = GQ.dma((lambda eg, slot: _dma3(wo_sb[slot], wo[eg, :, :, :], 512))(eg, slot))

            load_nope(0, 0)
            load_uv(0, 0)
            load_ukt(0, 0)

            d_qat = [None]
            for B in range(NB):
                nkB = 4 * (B + 1)
                b.waits("sync", [qat_B_free[0]])
                b.waits("sync", [qat_dma[4 * B + j] for j in range(4)])
                d_qat[0] = SQ.dma((lambda B: _dma3(
                    qat_B, qat_scr[:, :, B * 512:(B + 1) * 512], 512))(B))
                if B == 0:
                    load_wo(0, 0)
                def emit_qnope(h2, B=B):
                    hs2 = h2 % 2
                    b.waits("tensor", [d_nope[hs2], d_qat[0], ps_free[5]])
                    for lc in range(12):
                        b.op("tensor", lambda e, hs2=hs2, lc=lc: e.matmul(
                            PS[5][:, :], wnope_sb[hs2][:, lc * 128:(lc + 1) * 128],
                            qat_B[:, lc * 512:(lc + 1) * 512],
                            start=(lc == 0), stop=(lc == 11)),
                            T if lc == 11 else None)
                    t_qn = b.cnt[T]
                    if h2 < 15:
                        load_nope(h2 + 1, 1 - hs2)
                    elif B < NB - 1:
                        load_nope(0, 1 - hs2)
                    wnope_free[hs2] = (T, t_qn)
                    b.waits("scalar", [(T, t_qn), qnope_free[hs2]])
                    s_qn = b.op("scalar", lambda e, hs2=hs2: e.copy(
                        qnope_sb[hs2][:, :], PS[5][:, :]), Sc)
                    ps_free[5] = (Sc, s_qn)
                    return s_qn

                s_qn_tok = [None, None]
                for h in range(16):
                    hs = h % 2
                    emit_conversions(3)
                    if h == 0:
                        s_qn_tok[0] = emit_qnope(0)
                    s_qn = s_qn_tok[hs]
                    qab_banks = (3, 4, 5, 0)
                    b.waits("tensor", [d_ukt[hs], (Sc, s_qn)]
                            + [ps_free[j] for j in qab_banks])
                    for rc in range(4):
                        b.op("tensor", lambda e, hs=hs, rc=rc, qb=qab_banks: e.matmul(
                            PS[qb[rc]][:, :],
                            wukt_sb[hs][:, rc * 128:(rc + 1) * 128],
                            qnope_sb[hs][:, :], start=True, stop=True),
                            T if rc == 3 else None)
                    t_qab = b.cnt[T]
                    wukt_free[hs] = (T, t_qab)
                    if h < 15:
                        load_ukt(h + 1, 1 - hs)
                    elif B < NB - 1:
                        load_ukt(0, 1 - hs)
                    qnope_free[hs] = (T, t_qab)
                    b.waits("vector", [(T, t_qab), qf_free[hs]])
                    for rc in range(4):
                        b.op("vector", lambda e, hs=hs, rc=rc, qb=qab_banks: e.tensor_copy(
                            qf[hs][:, rc * 512:(rc + 1) * 512], PS[qb[rc]][:, :]),
                            V if rc == 3 else None)
                    v_qf = b.cnt[V]
                    for j in qab_banks:
                        ps_free[j] = (V, v_qf)

                    pt_evac_v = 0
                    last_S_T = 0
                    sbank = 0
                    prev_exp = None
                    pend_tr = None

                    def emit_transposes(i, n_k, v_pn):
                        nonlocal pt_evac_v
                        b.wait("tensor", V, v_pn)
                        t_tr_done = 0
                        for kc in range(n_k):
                            pb = (kc // 8) % 2
                            slot = kc % 8
                            if slot == 0:
                                b.waits("tensor", [pt_free[pb]])
                            b.op("tensor", lambda e, pb=pb, slot=slot, kc=kc, i=i,
                                 last=(slot == 7 or kc == n_k - 1): e.matmul(
                                PT[pb][:, slot * 128:(slot + 1) * 128],
                                p_nat[i % 2][:, kc * 128:(kc + 1) * 128], ident[:, :],
                                is_transpose=True, start=(slot == 0), stop=last),
                                T if (slot == 7 or kc == n_k - 1) else None)
                            if slot == 7 or kc == n_k - 1:
                                t_tr_done = b.cnt[T]
                                nsl = slot + 1
                                base = kc - slot
                                b.waits("vector", [(T, t_tr_done), ptasm_free])
                                b.op("vector", lambda e, pb=pb, nsl=nsl, base=base, i=i:
                                     e.tensor_copy(
                                         ptasm[:, :].rearrange(
                                             "p (a c) -> p a c", c=512)[
                                             :, base:base + nsl, i * 128:(i + 1) * 128],
                                         PT[pb][:, :].rearrange(
                                             "p (a c) -> p a c", c=128)[:, 0:nsl, :]),
                                     V)
                                pt_free[pb] = (V, b.cnt[V])
                        p_nat_free[i % 2] = (T, t_tr_done)
                        pt_evac_v = b.cnt[V]

                    for i in range(4):
                        qi = 4 * B + i
                        n_k = qi + 1
                        G = (n_k + 3) // 4
                        s_ev = {}
                        for g in range(G):
                            w = (min(4 * g + 4, n_k) - 4 * g) * 128
                            pk = 1 + (sbank % 3)
                            sbank += 1
                            bank = PS[pk]
                            b.waits("tensor", [(V, v_qf), ps_free[pk],
                                               (V, qpeT_ready), (V, kT_ready)])
                            for cc in range(5):
                                if cc < 4:
                                    def lhs(e, hs=hs, cc=cc, i=i):
                                        return qf[hs][:, cc * 512 + i * 128:cc * 512 + (i + 1) * 128]

                                    def rhs(e, cc=cc, g=g, w=w):
                                        return kT[cc][:, 4 * g * 128:4 * g * 128 + w]
                                else:
                                    def lhs(e, h=h, B=B, i=i):
                                        return qpeT[h // 2][:,
                                                            B * 512 + i * 128:B * 512 + (i + 1) * 128]

                                    def rhs(e, h=h, g=g, w=w):
                                        kk = kTpe_lo if h % 2 == 0 else kTpe_hi
                                        return kk[:, 4 * g * 128:4 * g * 128 + w]
                                b.op("tensor", lambda e, bank=bank, w=w, cc=cc,
                                     lhs=lhs, rhs=rhs: e.matmul(
                                         bank[:, 0:w], lhs(e), rhs(e),
                                         start=(cc == 0), stop=(cc == 4)),
                                     T if cc == 4 else None)
                            t_sg = b.cnt[T]
                            b.wait("scalar", T, t_sg)
                            if prev_exp is not None:
                                b.wait("scalar", Sc, prev_exp)
                            s_ev[g] = b.op("scalar", lambda e, g=g, w=w, bank=bank: e.copy(
                                s_stage[:, 4 * g * 128:4 * g * 128 + w], bank[:, 0:w]), Sc)
                            ps_free[pk] = (Sc, s_ev[g])
                        last_S_T = b.cnt[T]
                        gl = G - 1
                        doff = (qi) * 128
                        b.wait("vector", Sc, s_ev[gl])
                        v_mask = b.op("vector", lambda e, doff=doff: e.tensor_tensor(
                            s_stage[:, doff:doff + 128], s_stage[:, doff:doff + 128],
                            maskneg[:, :], ALU.add), V)
                        for g in range(G):
                            w = (min(4 * g + 4, n_k) - 4 * g) * 128
                            b.wait("vector", Sc, s_ev[g])
                            if g == gl:
                                b.wait("vector", V, v_mask)
                            b.op("vector", lambda e, g=g, w=w: e.reduce_max(
                                mx_part[:, g:g + 1], s_stage[:, 4 * g * 128:4 * g * 128 + w],
                                AX.X), V if g == G - 1 else None)
                        v_pm = b.cnt[V]
                        b.wait("vector", V, v_pm)
                        v_cm = b.op("vector", lambda e, G=G: e.reduce_max(
                            mx_row[:, :], mx_part[:, 0:G], AX.X), V)
                        b.wait("vector", V, v_cm)
                        v_bias = b.op("vector", lambda e: e.tensor_scalar(
                            e_bias[:, :], mx_row[:, :], -SM, None, ALU.mult), V)
                        b.waits("scalar", [(V, v_bias), p_nat_free[i % 2]])
                        for g in range(G):
                            w = (min(4 * g + 4, n_k) - 4 * g) * 128
                            b.op("scalar", lambda e, g=g, w=w, i=i: e.activation(
                                p_nat[i % 2][:, 4 * g * 128:4 * g * 128 + w],
                                s_stage[:, 4 * g * 128:4 * g * 128 + w],
                                AF.Exp, bias=e_bias[:, :], scale=SM,
                                accum_out=sm_sums[:, g:g + 1]),
                                Sc if g == G - 1 else None)
                        s_exp = b.cnt[Sc]
                        prev_exp = s_exp
                        b.wait("vector", Sc, s_exp)
                        v_st = b.op("vector", lambda e, G=G: e.reduce_sum(
                            sm_tot[:, :], sm_sums[:, 0:G], AX.X), V)
                        b.wait("vector", V, v_st)
                        v_si = b.op("vector", lambda e: e.reciprocal(
                            sm_inv[:, :], sm_tot[:, :]), V)
                        b.wait("vector", V, v_si)
                        v_pn = b.op("vector", lambda e, n_k=n_k, i=i: e.tensor_scalar(
                            p_nat[i % 2][:, 0:n_k * 128], p_nat[i % 2][:, 0:n_k * 128],
                            sm_inv[:, :], None, ALU.mult), V)
                        if pend_tr is not None:
                            emit_transposes(*pend_tr)
                        pend_tr = (i, n_k, v_pn)
                    if h < 15:
                        s_qn_tok[1 - hs] = emit_qnope(h + 1)
                    emit_transposes(*pend_tr)
                    qf_free[hs] = (T, last_S_T)

                    o_banks = (3, 4, 5, 0)
                    b.waits("tensor", [(V, pt_evac_v), (V, ckv_ready),
                                       t1_free if h == 0 else None]
                            + [ps_free[j] for j in o_banks])
                    for kc in range(nkB):
                        j0 = max(0, kc - 4 * B)
                        for rc in range(4):
                            b.op("tensor", lambda e, kc=kc, j0=j0, rc=rc, ob=o_banks,
                                 nkB=nkB: e.matmul(
                                     PS[ob[rc]][:, j0 * 128:512],
                                     ckv_nat[:, kc * 512 + rc * 128:kc * 512 + (rc + 1) * 128],
                                     ptasm[:, kc * 512 + j0 * 128:(kc + 1) * 512],
                                     start=(kc == 0), stop=(kc == nkB - 1)),
                                 T if (kc == nkB - 1 and rc == 3) else None)
                    t_o = b.cnt[T]
                    ptasm_free = (T, t_o)
                    b.waits("vector", [(T, t_o), o_sb_free])
                    b.waits("scalar", [(T, t_o), o_sb_free])
                    for rc in range(2):
                        b.op("vector", lambda e, rc=rc, ob=o_banks: e.tensor_copy(
                            o_sb[:, rc * 512:(rc + 1) * 512], PS[ob[rc]][:, :]),
                            V if rc == 1 else None)
                    v_o = b.cnt[V]
                    for rc in range(2, 4):
                        b.op("scalar", lambda e, rc=rc, ob=o_banks: e.copy(
                            o_sb[:, rc * 512:(rc + 1) * 512], PS[ob[rc]][:, :]),
                            Sc if rc == 3 else None)
                    s_o = b.cnt[Sc]
                    ps_free[o_banks[0]] = (V, v_o)
                    ps_free[o_banks[1]] = (V, v_o)
                    ps_free[o_banks[2]] = (Sc, s_o)
                    ps_free[o_banks[3]] = (Sc, s_o)
                    b.waits("tensor", [(V, v_o), (Sc, s_o), d_uv[hs], ps_free[1]])
                    for rc in range(4):
                        b.op("tensor", lambda e, hs=hs, rc=rc: e.matmul(
                            PS[1][:, :], wuv_sb[hs][:, rc * 128:(rc + 1) * 128],
                            o_sb[:, rc * 512:(rc + 1) * 512],
                            start=(rc == 0), stop=(rc == 3)),
                            T if rc == 3 else None)
                    t_t1 = b.cnt[T]
                    o_sb_free = (T, t_t1)
                    wuv_free[hs] = (T, t_t1)
                    if h < 15:
                        load_uv(h + 1, 1 - hs)
                    elif B < NB - 1:
                        load_uv(0, 1 - hs)
                    b.wait("scalar", T, t_t1)
                    s_t1 = b.op("scalar", lambda e, h=h: e.copy(
                        t1_sb[:, h * 512:(h + 1) * 512], PS[1][:, :]), Sc)
                    ps_free[1] = (Sc, s_t1)
                t1_all_ready = b.cnt[Sc]

                for eg in range(4):
                    egs = eg % 2
                    if eg < 3:
                        load_wo(eg + 1, 1 - egs)
                    elif B < NB - 1:
                        load_wo(0, 1 - egs)
                    for tt4 in range(4):
                        bank = PS[1 + (tt4 % 2)]
                        b.waits("tensor", [d_wo[egs], (Sc, t1_all_ready),
                                           ps_free[1 + (tt4 % 2)]])
                        for hh in range(16):
                            b.op("tensor", lambda e, hh=hh, tt4=tt4, egs=egs, bank=bank:
                                 e.matmul(
                                     bank[:, :],
                                     t1_sb[:, hh * 512 + tt4 * 128:hh * 512 + (tt4 + 1) * 128],
                                     wo_sb[egs][:, hh * 512:(hh + 1) * 512],
                                     start=(hh == 0), stop=(hh == 15)),
                                 T if hh == 15 else None)
                        t_y = b.cnt[T]
                        if tt4 == 3:
                            wo_free[egs] = (T, t_y)
                        ysl = tt4 % 2
                        b.waits("scalar", [(T, t_y), y_free[ysl]])
                        s_y = b.op("scalar", lambda e, ysl=ysl, bank=bank: e.copy(
                            y_sb[ysl][:, :], bank[:, :]), Sc)
                        ps_free[1 + (tt4 % 2)] = (Sc, s_y)
                        b.wait("sync", Sc, s_y)
                        ya_dma = SQ.dma(lambda e, B=B, tt4=tt4, eg=eg, ysl=ysl:
                                      e.dma_start(
                                          ya_scr[B * 512 + tt4 * 128:B * 512 + (tt4 + 1) * 128,
                                                 eg * 512:(eg + 1) * 512],
                                          y_sb[ysl][:, :]))
                        y_free[ysl] = ya_dma
                qat_B_free[0] = (T, b.cnt[T])
                ya_done[B] = ya_dma
                t1_free = (T, b.cnt[T])

            if PHASES <= 3:
                SQ.drain()
                GQ.drain()
            SQ.drain()
            GQ.drain()
            b.flush("ATT")
        attres.close()
        if PHASES <= 3:
            return nc

        with ExitStack() as ph:
            psb = lambda n, sh, dt: ph.enter_context(nc.sbuf_tensor(n, sh, dt))
            xnt_g = [psb(f"fx{i}", [128, 16 * 512], BF16) for i in range(2)]
            h_sb = psb("h_sb", [128, 64 * 512], BF16)
            wg_sb = [psb(f"wg{i}", [128, 16 * 128], BF16) for i in range(4)]
            wd_sb = [psb(f"wd{i}", [128, 4 * 512], BF16) for i in range(2)]
            sil_sb = [psb(f"sil{i}", [128, 512], F32) for i in range(2)]
            xr_sb = psb("xr_sb", [128, 16 * 512], F32)
            yr_sb = psb("yr_sb", [128, 16 * 512], F32)
            ydn_sb = psb("ydn_sb", [128, 4 * 512], F32)
            out_sb = [psb(f"os{i}", [128, 512], F32) for i in range(2)]

            xnt_g_free = [None, None]
            wg_free = [None] * 4
            wd_free = [None, None]
            sil_free = [None, None]
            xr_free = None
            yr_free = None
            ydn_free = None
            d_xr = None
            d_yr = None
            out_free = [None, None]
            h_free_T = 0
            d_g = [None] * 4
            d_d = [None, None]
            out_dma = None

            def load_gate_tile(j, kind, slot):
                oc = j + (64 if kind else 0)
                b.waits("sync", [wg_free[slot]])
                d_g[slot] = SQ.dma((lambda oc, slot: _dma3(wg_sb[slot], wgup_bf[oc, :, :, :], 128))(oc, slot))

            def load_down(eg, ic0, slot):
                b.waits("sync", [wd_free[slot]])
                d_d[slot] = SQ.dma((lambda eg, ic0, slot: _dma3(wd_sb[slot], wdown_bf[eg, ic0:ic0 + 4, :, :], 512))(eg, ic0, slot))

            for g in range(4):
                gs = g % 2
                b.waits("sync", [xnt_g_free[gs]])
                b.waits("sync", list(xnt_dma))
                d_xg = SQ.dma((lambda g, gs: _dma3(xnt_g[gs], xnt_scr[:, :, g * 512:(g + 1) * 512], 512))(g, gs))
                if g == 0:
                    load_gate_tile(0, 0, 0)
                    load_gate_tile(0, 1, 1)
                for j in range(64):
                    s_g = 2 * j % 4
                    s_u = (2 * j + 1) % 4
                    if j < 63:
                        load_gate_tile(j + 1, 0, (2 * j + 2) % 4)
                        load_gate_tile(j + 1, 1, (2 * j + 3) % 4)
                    elif g < 3:
                        load_gate_tile(0, 0, (2 * j + 2) % 4)
                        load_gate_tile(0, 1, (2 * j + 3) % 4)
                    if j == 8:
                        b.waits("gpsimd", [xr_free, yr_free, ya_done[g]])
                        d_res = []
                        for tt4 in range(4):
                            d_res.append(GQ.dma(lambda e, g=g, tt4=tt4: e.dma_start(
                                xr_sb[:, tt4 * 2048:(tt4 + 1) * 2048],
                                x_d[g * 512 + tt4 * 128:g * 512 + (tt4 + 1) * 128, :])))
                            d_res.append(GQ.dma(lambda e, g=g, tt4=tt4: e.dma_start(
                                yr_sb[:, tt4 * 2048:(tt4 + 1) * 2048],
                                ya_scr[g * 512 + tt4 * 128:g * 512 + (tt4 + 1) * 128, :])))
                    pg, pu = PS[(j % 2) * 2], PS[(j % 2) * 2 + 1]
                    b.waits("tensor", [d_g[s_g], d_xg,
                                       ps_free[(j % 2) * 2]])
                    for ec in range(16):
                        b.op("tensor", lambda e, gs=gs, ec=ec, s_g=s_g, pg=pg: e.matmul(
                            pg[:, :], wg_sb[s_g][:, ec * 128:(ec + 1) * 128],
                            xnt_g[gs][:, ec * 512:(ec + 1) * 512],
                            start=(ec == 0), stop=(ec == 15)),
                            T if ec == 15 else None)
                    t_g = b.cnt[T]
                    b.waits("tensor", [d_g[s_u], ps_free[(j % 2) * 2 + 1]])
                    for ec in range(16):
                        b.op("tensor", lambda e, gs=gs, ec=ec, s_u=s_u, pu=pu: e.matmul(
                            pu[:, :], wg_sb[s_u][:, ec * 128:(ec + 1) * 128],
                            xnt_g[gs][:, ec * 512:(ec + 1) * 512],
                            start=(ec == 0), stop=(ec == 15)),
                            T if ec == 15 else None)
                    t_u = b.cnt[T]
                    wg_free[s_g] = (T, t_g)
                    wg_free[s_u] = (T, t_u)
                    ssl = j % 2
                    b.waits("scalar", [(T, t_g), sil_free[ssl]])
                    s_si = b.op("scalar", lambda e, pg=pg, ssl=ssl: e.activation(
                        sil_sb[ssl][:, :], pg[:, :], AF.Silu), Sc)
                    ps_free[(j % 2) * 2] = (Sc, s_si)
                    b.waits("vector", [(Sc, s_si), (T, t_u),
                                       (T, h_free_T) if h_free_T else None])
                    v_h = b.op("vector", lambda e, j=j, ssl=ssl, pu=pu: e.tensor_tensor(
                        h_sb[:, j * 512:(j + 1) * 512], sil_sb[ssl][:, :], pu[:, :],
                        ALU.mult), V)
                    sil_free[ssl] = (V, v_h)
                    ps_free[(j % 2) * 2 + 1] = (V, v_h)
                h_ready = b.cnt[V]
                xnt_g_free[gs] = (T, b.cnt[T])

                for eg in range(4):
                    if g == 0 and eg == 0:
                        load_down(0, 0, 0)
                    banks = [PS[2 + k] for k in range(4)]
                    b.waits("tensor", [(V, h_ready)]
                            + [ps_free[2 + k] for k in range(4)])
                    for ic in range(64):
                        isl = (ic // 4) % 2
                        if ic % 4 == 0:
                            nic = ic + 4
                            neg, ng2 = eg, g
                            if nic >= 64:
                                nic = 0
                                neg = eg + 1
                                if neg >= 4:
                                    neg, ng2 = 0, g + 1
                            if ng2 < 4:
                                load_down(neg, nic, 1 - isl)
                        b.waits("tensor", [d_d[isl]])
                        for tt4 in range(4):
                            b.op("tensor", lambda e, ic=ic, tt4=tt4, isl=isl,
                                 banks=banks: e.matmul(
                                     banks[tt4][:, :],
                                     h_sb[:, ic * 512 + tt4 * 128:ic * 512 + (tt4 + 1) * 128],
                                     wd_sb[isl][:, (ic % 4) * 512:(ic % 4 + 1) * 512],
                                     start=(ic == 0), stop=(ic == 63)),
                                 T if (tt4 == 3 and ic % 4 == 3) else None)
                        if ic % 4 == 3:
                            wd_free[isl] = (T, b.cnt[T])
                    t_dn = b.cnt[T]
                    b.waits("scalar", [(T, t_dn), ydn_free])
                    for tt4 in range(4):
                        b.op("scalar", lambda e, tt4=tt4, banks=banks: e.copy(
                            ydn_sb[:, tt4 * 512:(tt4 + 1) * 512],
                            banks[tt4][:, :]), Sc if tt4 == 3 else None)
                    s_dn = b.cnt[Sc]
                    for k in range(4):
                        ps_free[2 + k] = (Sc, s_dn)
                    for tt4 in range(4):
                        osl = tt4 % 2
                        b.waits("vector", d_res + [(Sc, s_dn),
                                           out_free[osl]])
                        v_o1 = b.op("vector", lambda e, tt4=tt4, eg=eg, osl=osl:
                                    e.tensor_tensor(
                                        out_sb[osl][:, :],
                                        xr_sb[:, tt4 * 2048 + eg * 512:tt4 * 2048 + (eg + 1) * 512],
                                        yr_sb[:, tt4 * 2048 + eg * 512:tt4 * 2048 + (eg + 1) * 512],
                                        ALU.add), V)
                        b.wait("vector", V, v_o1)
                        v_ob = b.op("vector", lambda e, osl=osl, tt4=tt4:
                                    e.tensor_tensor(
                                        out_sb[osl][:, :], out_sb[osl][:, :],
                                        ydn_sb[:, tt4 * 512:(tt4 + 1) * 512],
                                        ALU.add), V)
                        b.wait("sync", V, v_ob)
                        out_dma = SQ.dma(lambda e, g=g, tt4=tt4, eg=eg, osl=osl:
                                       e.dma_start(
                                           out_d[g * 512 + tt4 * 128:g * 512 + (tt4 + 1) * 128,
                                                 eg * 512:(eg + 1) * 512],
                                           out_sb[osl][:, :]))
                        out_free[osl] = out_dma
                    v_cmb = b.cnt[V]
                    ydn_free = (V, v_cmb)
                    if eg == 3:
                        xr_free = (V, v_cmb)
                        yr_free = (V, v_cmb)
                h_free_T = b.cnt[T]

            b.waits("sync", [out_dma])
            SQ.drain()
            GQ.drain()
            b.flush("FFN")

    return nc


def _host_inputs(inp):
    x = np.ascontiguousarray(np.asarray(inp["L0_attn_x"], dtype=np.float32))
    w_q_a = np.asarray(inp["L0_attn_w_q_a"], dtype=np.float32)
    lnq = np.asarray(inp["L0_attn_q_a_ln_w"], dtype=np.float32)
    w_q_b = np.asarray(inp["L0_attn_w_q_b"], dtype=np.float32)
    w_kv_a = np.asarray(inp["L0_attn_w_kv_a"], dtype=np.float32)
    lnkv = np.asarray(inp["L0_attn_kv_a_ln_w"], dtype=np.float32)
    w_kv_b = np.asarray(inp["L0_attn_w_kv_b"], dtype=np.float32)
    w_o = np.asarray(inp["L0_attn_w_o"], dtype=np.float32)
    w_gup = np.asarray(inp["L0_ffn_w_gate_up"], dtype=np.float32)
    w_down = np.asarray(inp["L0_ffn_w_down"], dtype=np.float32)

    com = {}
    com["wqa"] = np.ascontiguousarray(w_q_a.reshape(16, 128, QLR))
    com["wkva"] = np.ascontiguousarray(w_kv_a.reshape(16, 128, R + DR))
    qb = w_q_b.reshape(QLR, H, DN + DR)
    com["wqbpe"] = np.ascontiguousarray(
        qb[:, :, DN:].reshape(QLR, H * DR).reshape(12, 128, H * DR))
    com["wqbnope"] = np.ascontiguousarray(
        qb[:, :, :DN].reshape(12, 128, H, DN).transpose(2, 0, 1, 3))
    kvb = w_kv_b.reshape(R, H, DN + DV)
    com["wukt"] = np.ascontiguousarray(kvb[:, :, :DN].transpose(1, 2, 0))
    com["wuv"] = np.ascontiguousarray(
        kvb[:, :, DN:].reshape(4, 128, H, DV).transpose(2, 0, 1, 3))
    com["wo"] = np.ascontiguousarray(
        w_o.reshape(H, 128, 4, 512).transpose(2, 0, 1, 3))
    com["wgup"] = np.ascontiguousarray(
        w_gup.reshape(128, 128, 16, 128).transpose(0, 2, 3, 1))
    com["wdown"] = np.ascontiguousarray(
        w_down.reshape(4, 512, 64, 128).transpose(0, 2, 3, 1))
    com["lnq"] = np.ascontiguousarray(lnq.reshape(1, QLR))
    com["lnkv"] = np.ascontiguousarray(lnkv.reshape(1, R))

    t = np.arange(S, dtype=np.float64)
    inv_freq = 1.0 / (10000.0 ** (np.arange(0, DR, 2, dtype=np.float64) / DR))
    ang = t[:, None] * inv_freq[None, :]
    cos32 = np.cos(ang).astype(np.float32)
    sin32 = np.sin(ang).astype(np.float32)
    cos64 = np.concatenate([cos32, cos32], axis=1)
    sinp = np.concatenate([-sin32, sin32], axis=1)
    com["cosx2"] = np.ascontiguousarray(
        np.concatenate([cos64, cos64], axis=1).reshape(16, 128, 128))
    com["sinx2"] = np.ascontiguousarray(
        np.concatenate([sinp, sinp], axis=1).reshape(16, 128, 128))
    com["ident"] = np.eye(128, dtype=np.float32)
    m = np.zeros((128, 128), dtype=np.float32)
    m[np.triu_indices(128, 1)] = NEG
    com["maskneg"] = m
    com["ones1"] = np.ones((1, 128), dtype=np.float32)

    maps = []
    for c in range(8):
        d = dict(com)
        d["x"] = np.ascontiguousarray(x[c])
        maps.append(d)
    return maps


def kernel(**inputs):
    maps = _host_inputs(inputs)
    nc = build_nc()
    res = run_bass_kernel_spmd(nc, maps, core_ids=list(range(8)))
    return np.stack([np.asarray(r["out"], dtype=np.float32) for r in res.results], axis=0)


# version 15
# speedup vs baseline: 1.0284x; 1.0197x over previous
import sys

import numpy as np

try:
    import concourse.bass as bass
except ImportError:
    sys.path.insert(0, "/opt/trn_rl_repo")
    import concourse.bass as bass
import concourse.mybir as mybir
from concourse.bass_utils import run_bass_kernel_spmd
from contextlib import ExitStack

F32 = mybir.dt.float32
BF16 = mybir.dt.bfloat16
AF = mybir.ActivationFunctionType
ALU = mybir.AluOpType
AX = mybir.AxisListType

S = 2048
E = 2048
H = 16
DN, DR, DV, R, QLR = 128, 64, 128, 512, 1536
SM = float((DN + DR) ** -0.5)
EPS_LN = 1e-5
EPS_RMS = 1e-6
NT = S // 128
NB = S // 512
NEG = -1.0e30

DEBUG = False
PHASES = 4
P1STAGE = 9


class Bld:
    ENGS = ("tensor", "vector", "scalar", "gpsimd", "sync")

    def __init__(self, nc):
        self.nc = nc
        self.q = {e: [] for e in self.ENGS}
        self.sems = {}
        self.cnt = {}
        self.hw = {e: {} for e in self.ENGS}

    def new_sem(self, st, name):
        self.sems[name] = st.enter_context(self.nc.semaphore(name))
        self.cnt[name] = 0
        return name

    def op(self, eng, fn, inc=None, amt=1):
        if inc is None:
            self.q[eng].append(lambda e, fn=fn: fn(e))
            return None
        h = self.sems[inc]
        self.q[eng].append(lambda e, fn=fn, h=h, a=amt: fn(e).then_inc(h, a))
        self.cnt[inc] += amt
        return self.cnt[inc]

    def wait(self, eng, sem, val):
        if val is None or val <= 0:
            return
        assert val <= self.cnt[sem], f"wait-on-future {sem} {val}>{self.cnt[sem]}"
        if self.hw[eng].get(sem, 0) >= val:
            return
        self.hw[eng][sem] = val
        h = self.sems[sem]
        self.q[eng].append(lambda e, h=h, v=val: e.wait_ge(h, v))

    def waits(self, eng, deps):
        best = {}
        for p in deps:
            if not p:
                continue
            if isinstance(p, list):
                items = p
            else:
                items = [p]
            for s, v in items:
                if v and (s not in best or v > best[s]):
                    best[s] = v
        for s, v in best.items():
            self.wait(eng, s, v)

    def flush(self, name):
        with self.nc.Block(name) as blk:
            for eng in self.ENGS:
                if self.q[eng]:
                    getattr(blk, eng)(lambda e, lst=list(self.q[eng]): [f(e) for f in lst])
        self.q = {e: [] for e in self.ENGS}



class DmaQ:

    def __init__(self, b, st, eng, name, k=8):
        self.b = b
        self.eng = eng
        self.k = k
        self.names = [b.new_sem(st, f"{name}{i}") for i in range(k)]
        self.n = 0

    def dma(self, fn):
        b = self.b
        i = self.n
        self.n += 1
        slot = i % self.k
        sem = self.names[slot]
        uses = i // self.k
        if uses > 0:
            b.wait(self.eng, sem, 16 * uses)
        cnt = b.op(self.eng, fn, sem, 16)
        return (sem, cnt)

    def drain(self):
        for sem in self.names:
            self.b.wait(self.eng, sem, self.b.cnt[sem])


def _dma3(dst, src3, c):
    return lambda e: e.dma_start(
        dst[:, :].rearrange("p (a c) -> p a c", c=c),
        src3.rearrange("a b c -> b a c"))


def _dma3_out(dst3, src, c):
    return lambda e: e.dma_start(
        dst3.rearrange("a b c -> b a c"),
        src[:, :].rearrange("p (a c) -> p a c", c=c))


def build_nc():
    nc = bass.Bass("TRN2", target_bir_lowering=False)

    x_d = nc.dram_tensor("x", [S, E], F32, kind="ExternalInput").ap()
    wqa = nc.dram_tensor("wqa", [16, 128, QLR], F32, kind="ExternalInput").ap()
    wkva = nc.dram_tensor("wkva", [16, 128, R + DR], F32, kind="ExternalInput").ap()
    wqbpe = nc.dram_tensor("wqbpe", [12, 128, H * DR], F32, kind="ExternalInput").ap()
    wqbnope = nc.dram_tensor("wqbnope", [H, 12, 128, DN], F32, kind="ExternalInput").ap()
    wukt = nc.dram_tensor("wukt", [H, 128, R], F32, kind="ExternalInput").ap()
    wuv = nc.dram_tensor("wuv", [H, 4, 128, DV], F32, kind="ExternalInput").ap()
    wo = nc.dram_tensor("wo", [4, H, 128, 512], F32, kind="ExternalInput").ap()
    wgup = nc.dram_tensor("wgup", [128, 16, 128, 128], F32, kind="ExternalInput").ap()
    wdown = nc.dram_tensor("wdown", [4, 64, 128, 512], F32, kind="ExternalInput").ap()
    lnq_d = nc.dram_tensor("lnq", [1, QLR], F32, kind="ExternalInput").ap()
    lnkv_d = nc.dram_tensor("lnkv", [1, R], F32, kind="ExternalInput").ap()
    cosx2_d = nc.dram_tensor("cosx2", [16, 128, 128], F32, kind="ExternalInput").ap()
    sinx2_d = nc.dram_tensor("sinx2", [16, 128, 128], F32, kind="ExternalInput").ap()
    ident_d = nc.dram_tensor("ident", [128, 128], F32, kind="ExternalInput").ap()
    mask_d = nc.dram_tensor("maskneg", [128, 128], F32, kind="ExternalInput").ap()
    ones_d = nc.dram_tensor("ones1", [1, 128], F32, kind="ExternalInput").ap()

    out_d = nc.dram_tensor("out", [S, E], F32, kind="ExternalOutput").ap()
    scratch_kind = "ExternalOutput" if DEBUG else "Internal"
    xnt_scr = nc.dram_tensor("xnt_scr", [16, 128, S], BF16, kind=scratch_kind).ap()
    qat_scr = nc.dram_tensor("qat_scr", [12, 128, S], BF16, kind=scratch_kind).ap()
    ya_scr = nc.dram_tensor("ya_scr", [S, E], F32, kind=scratch_kind).ap()
    wgup_bf = nc.dram_tensor("wgup_bf", [128, 16, 128, 128], BF16, kind="Internal").ap()
    wdown_bf = nc.dram_tensor("wdown_bf", [4, 64, 128, 512], BF16, kind="Internal").ap()

    b = Bld(nc)

    with ExitStack() as outer:
        sb = lambda n, sh, dt: outer.enter_context(nc.sbuf_tensor(n, sh, dt))
        SQ = DmaQ(b, outer, "sync", "Ds", 8)
        GQ = DmaQ(b, outer, "gpsimd", "Dg", 8)
        T = b.new_sem(outer, "T")
        V = b.new_sem(outer, "V")
        Sc = b.new_sem(outer, "Sc")

        PS = [outer.enter_context(nc.psum_tensor(f"ps{i}", [128, 512], F32)) for i in range(6)]
        PT = [outer.enter_context(nc.psum_tensor(f"pt{i}", [128, 1024], BF16)) for i in range(2)]
        ps_free = [None] * 6
        pt_free = [None] * 2

        ident = sb("ident_sb", [128, 128], BF16)
        maskneg = sb("maskneg_sb", [128, 128], F32)
        attres = ExitStack()
        asb = lambda n, sh, dt: attres.enter_context(nc.sbuf_tensor(n, sh, dt))
        kT = [asb(f"kT{i}", [128, S], BF16) for i in range(4)]
        kTpe_lo = asb("kTpe_lo", [128, S], BF16)
        kTpe_hi = asb("kTpe_hi", [128, S], BF16)
        ckv_nat = asb("ckv_nat", [128, 16 * 512], BF16)
        qpeT = [asb(f"qpeT{i}", [128, S], BF16) for i in range(8)]

        prep = ExitStack()
        psb0 = lambda n, sh, dt: prep.enter_context(nc.sbuf_tensor(n, sh, dt))
        cosx2 = psb0("cosx2_sb", [128, 16 * 128], F32)
        sinx2 = psb0("sinx2_sb", [128, 16 * 128], F32)
        lnq_rep = psb0("lnq_rep", [128, QLR], F32)
        lnkv_rep = psb0("lnkv_rep", [128, R], F32)
        lnq_row = psb0("lnq_row", [1, QLR], F32)
        lnkv_row = psb0("lnkv_row", [1, R], F32)
        ones_row = psb0("ones_row", [1, 128], F32)

        with ExitStack() as ph:
            psb = lambda n, sh, dt: ph.enter_context(nc.sbuf_tensor(n, sh, dt))
            x_t = [psb(f"x_t{i}", [128, E], F32) for i in range(2)]
            xn_t = [psb(f"xn_t{i}", [128, E], BF16) for i in range(2)]
            xnt_t = [psb(f"xnt_t{i}", [128, E], BF16) for i in range(2)]
            wkva_sb = psb("wkva_sb", [128, 16 * (R + DR)], BF16)
            stat6 = psb("stat6", [128, 24], F32)
            stat2 = psb("stat2", [128, 2], F32)
            inv_s = psb("inv_s", [128, 1], F32)
            nbias = psb("nbias", [128, 1], F32)
            rs_sb = psb("rs_sb", [128, 1], F32)
            vk_sb = psb("vk_sb", [128, 1], F32)
            invk = psb("invk", [128, 1], F32)
            ckv_f = psb("ckv_f", [128, R], F32)
            km1 = psb("km1", [128, 64], F32)
            km2 = psb("km2", [128, 64], F32)
            krop_lo = psb("krop_lo", [128, 256], BF16)
            krop_hi = psb("krop_hi", [128, 256], BF16)

            d_mask = SQ.dma(lambda e: e.dma_start(maskneg[:, :], mask_d[:, :]))
            d_lnq = SQ.dma(lambda e: e.dma_start(lnq_row[:, :], lnq_d[:, :]))
            d_lnkv = SQ.dma(lambda e: e.dma_start(lnkv_row[:, :], lnkv_d[:, :]))
            d_ones = SQ.dma(lambda e: e.dma_start(ones_row[:, :], ones_d[:, :]))
            d_cosA = SQ.dma(_dma3(cosx2, cosx2_d[:, :, :], 128))
            d_cosB = SQ.dma(_dma3(sinx2, sinx2_d[:, :, :], 128))
            d_cos = [d_cosA, d_cosB]
            d_ident = GQ.dma(lambda e: e.dma_start(ident[:, :], ident_d[:, :]))
            d_wkva = GQ.dma(_dma3(wkva_sb, wkva[:, :, :], 576))

            b.waits("tensor", [d_lnq, d_lnkv, d_ones])
            for j in range(3):
                b.op("tensor", lambda e, j=j: e.matmul(
                    PS[j][:, :], ones_row[:, :], lnq_row[:, j * 512:(j + 1) * 512],
                    start=True, stop=True), T if j == 2 else None)
            b.op("tensor", lambda e: e.matmul(
                PS[3][:, :], ones_row[:, :], lnkv_row[:, :], start=True, stop=True), T)
            t_rep = b.cnt[T]
            b.wait("vector", T, t_rep)
            for j in range(3):
                b.op("vector", lambda e, j=j: e.tensor_copy(
                    lnq_rep[:, j * 512:(j + 1) * 512], PS[j][:, :]))
            v_rep = b.op("vector", lambda e: e.tensor_copy(lnkv_rep[:, :], PS[3][:, :]), V)
            for j in range(4):
                ps_free[j] = (V, v_rep)

            b.op("vector", lambda e: e.memset(krop_lo[:, 64:128], 0.0))
            b.op("vector", lambda e: e.memset(krop_lo[:, 192:256], 0.0))
            b.op("vector", lambda e: e.memset(krop_hi[:, 0:64], 0.0))
            v_ms = b.op("vector", lambda e: e.memset(krop_hi[:, 128:192], 0.0), V)
            xnt_dma = [None] * NT
            x_free = [None, None]
            xn_free = [None, None]
            xnt_free = [None, None]
            ckv_ready = 0

            d_x_t = [None] * (NT + 1)
            sxn_t = [None] * NT
            txn_t = [None] * NT
            vxnt_t = [None] * NT
            tkv_t = [None] * NT
            vck_t = [None] * NT
            vkp_t = [None] * NT
            tf_t = [None] * NT

            def p1_load(i):
                if i >= NT:
                    return
                b.waits("sync", [x_free[i % 2]])
                d_x_t[i] = SQ.dma(lambda e, i=i: e.dma_start(
                    x_t[i % 2][:, :], x_d[i * 128:(i + 1) * 128, :]))

            def p1_ln(i):
                sl = i % 2
                b.waits("vector", [d_x_t[i]])
                for sg in range(4):
                    b.op("vector", lambda e, sl=sl, sg=sg: e.bn_stats(
                        stat6[:, sg * 6:(sg + 1) * 6],
                        x_t[sl][:, sg * 512:(sg + 1) * 512]),
                        V if sg == 3 else None)
                v1 = b.cnt[V]
                b.wait("vector", V, v1)
                v2 = b.op("vector", lambda e: e.bn_aggr(stat2[:, :], stat6[:, :]), V)
                b.wait("vector", V, v2)
                v2b = b.op("vector", lambda e: e.tensor_scalar_add(
                    stat2[:, 1:2], stat2[:, 1:2], EPS_LN), V)
                b.wait("scalar", V, v2b)
                s1 = b.op("scalar", lambda e: e.activation(
                    inv_s[:, :], stat2[:, 1:2], AF.Ln), Sc)
                b.wait("scalar", Sc, s1)
                s2 = b.op("scalar", lambda e: e.activation(
                    inv_s[:, :], inv_s[:, :], AF.Exp, scale=-0.5), Sc)
                b.wait("vector", Sc, s2)
                v3 = b.op("vector", lambda e: e.scalar_tensor_tensor(
                    nbias[:, :], stat2[:, 0:1], -1.0, inv_s[:, :], ALU.mult, ALU.mult), V)
                b.waits("vector", [(V, v3), (T, txn_t[i - 2]) if i >= 2 else None])
                sxn_t[i] = b.op("vector", lambda e, sl=sl: e.tensor_scalar(
                    xn_t[sl][:, :], x_t[sl][:, :], inv_s[:, :], nbias[:, :],
                    ALU.mult, ALU.add), V)
                x_free[sl] = [(V, sxn_t[i])]

            def p1_txn(i):
                sl = i % 2
                b.waits("tensor", [(V, sxn_t[i]), d_ident])
                b.waits("vector", [xnt_free[sl]])
                for half in range(2):
                    pb = half
                    b.waits("tensor", [pt_free[pb]])
                    for j in range(8):
                        ec = half * 8 + j
                        b.op("tensor", lambda e, sl=sl, pb=pb, j=j, ec=ec: e.matmul(
                            PT[pb][:, j * 128:(j + 1) * 128],
                            xn_t[sl][:, ec * 128:(ec + 1) * 128], ident[:, :],
                            is_transpose=True, start=(j == 0), stop=(j == 7)),
                            T if j == 7 else None)
                    t_tr = b.cnt[T]
                    b.wait("vector", T, t_tr)
                    v_ev = b.op("vector", lambda e, sl=sl, half=half, pb=pb: e.tensor_copy(
                        xnt_t[sl][:, half * 1024:(half + 1) * 1024], PT[pb][:, :]), V)
                    pt_free[pb] = (V, v_ev)
                txn_t[i] = b.cnt[T]
                vxnt_t[i] = b.cnt[V]
                xn_free[sl] = (V, vxnt_t[i])
                b.wait("sync", V, vxnt_t[i])
                xnt_dma[i] = SQ.dma((lambda sl, i: _dma3_out(
                    xnt_scr[:, :, i * 128:(i + 1) * 128], xnt_t[sl], 128))(sl, i))

            def p1_kv(i):
                sl = i % 2
                b.waits("tensor", [d_wkva, ps_free[4], ps_free[5], (V, vxnt_t[i])])
                for ec in range(16):
                    b.op("tensor", lambda e, sl=sl, ec=ec: e.matmul(
                        PS[4][:, :], xnt_t[sl][:, ec * 128:(ec + 1) * 128],
                        wkva_sb[:, ec * 576:ec * 576 + 512],
                        start=(ec == 0), stop=(ec == 15)),
                        T if ec == 15 else None)
                for ec in range(16):
                    b.op("tensor", lambda e, sl=sl, ec=ec: e.matmul(
                        PS[5][:, 0:64], xnt_t[sl][:, ec * 128:(ec + 1) * 128],
                        wkva_sb[:, ec * 576 + 512:ec * 576 + 576],
                        start=(ec == 0), stop=(ec == 15)),
                        T if ec == 15 else None)
                tkv_t[i] = b.cnt[T]
                xnt_free[sl] = [(T, tkv_t[i]), xnt_dma[i]]

            def p1_norm(i):
                b.waits("scalar", [(T, tkv_t[i]),
                                   (V, vck_t[i - 1]) if i >= 1 else None])
                s3 = b.op("scalar", lambda e: e.square(ckv_f[:, :], PS[4][:, :]), Sc)
                b.wait("vector", Sc, s3)
                v4 = b.op("vector", lambda e: e.reduce_sum(rs_sb[:, :], ckv_f[:, :], AX.X), V)
                b.wait("vector", V, v4)
                v5 = b.op("vector", lambda e: e.tensor_scalar(
                    vk_sb[:, :], rs_sb[:, :], 1.0 / R, EPS_RMS, ALU.mult, ALU.add), V)
                b.wait("scalar", V, v5)
                s4 = b.op("scalar", lambda e: e.activation(invk[:, :], vk_sb[:, :], AF.Ln), Sc)
                b.wait("scalar", Sc, s4)
                s5 = b.op("scalar", lambda e: e.activation(
                    invk[:, :], invk[:, :], AF.Exp, scale=-0.5), Sc)
                b.wait("scalar", Sc, s5)
                s6 = b.op("scalar", lambda e: e.activation(
                    ckv_f[:, :], PS[4][:, :], AF.Copy, scale=invk[:, :]), Sc)
                ps_free[4] = (Sc, s6)
                b.wait("vector", Sc, s6)
                vck_t[i] = b.op("vector", lambda e, i=i: e.tensor_tensor(
                    ckv_nat[:, i * 512:(i + 1) * 512], ckv_f[:, :], lnkv_rep[:, :],
                    ALU.mult), V)
                b.waits("vector", [(T, tkv_t[i])] + d_cos
                        + [(T, tf_t[i - 2]) if i >= 2 else None])
                b.op("vector", lambda e, i=i: e.tensor_tensor(
                    km1[:, :], PS[5][:, 0:64], cosx2[:, i * 128:i * 128 + 64], ALU.mult))
                b.op("vector", lambda e, i=i: e.tensor_tensor(
                    km2[:, 0:32], PS[5][:, 32:64], sinx2[:, i * 128:i * 128 + 32], ALU.mult))
                v7 = b.op("vector", lambda e, i=i: e.tensor_tensor(
                    km2[:, 32:64], PS[5][:, 0:32], sinx2[:, i * 128 + 32:i * 128 + 64],
                    ALU.mult), V)
                b.wait("vector", V, v7)
                ksl = (i % 2) * 128
                b.op("vector", lambda e, ksl=ksl: e.tensor_tensor(
                    krop_lo[:, ksl:ksl + 64], km1[:, :], km2[:, :], ALU.add))
                vkp_t[i] = b.op("vector", lambda e, ksl=ksl: e.tensor_tensor(
                    krop_hi[:, ksl + 64:ksl + 128], km1[:, :], km2[:, :], ALU.add), V)
                ps_free[5] = (V, vkp_t[i])

            def p1_tk(i):
                b.waits("tensor", [(V, vck_t[i]), (V, vkp_t[i]), (V, v_ms),
                                   pt_free[0]])
                for rc in range(4):
                    b.op("tensor", lambda e, i=i, rc=rc: e.matmul(
                        PT[0][:, rc * 128:(rc + 1) * 128],
                        ckv_nat[:, i * 512 + rc * 128:i * 512 + (rc + 1) * 128],
                        ident[:, :], is_transpose=True, start=(rc == 0), stop=False))
                ksl = (i % 2) * 128
                b.op("tensor", lambda e, ksl=ksl: e.matmul(
                    PT[0][:, 512:640], krop_lo[:, ksl:ksl + 128], ident[:, :],
                    is_transpose=True, start=False, stop=False))
                t_k5 = b.op("tensor", lambda e, ksl=ksl: e.matmul(
                    PT[0][:, 640:768], krop_hi[:, ksl:ksl + 128], ident[:, :],
                    is_transpose=True, start=False, stop=True), T)
                tf_t[i] = t_k5
                b.wait("vector", T, t_k5)
                for rc in range(4):
                    b.op("vector", lambda e, i=i, rc=rc: e.tensor_copy(
                        kT[rc][:, i * 128:(i + 1) * 128], PT[0][:, rc * 128:(rc + 1) * 128]))
                b.op("vector", lambda e, i=i: e.tensor_copy(
                    kTpe_lo[:, i * 128:(i + 1) * 128], PT[0][:, 512:640]))
                v9 = b.op("vector", lambda e, i=i: e.tensor_copy(
                    kTpe_hi[:, i * 128:(i + 1) * 128], PT[0][:, 640:768]), V)
                pt_free[0] = (V, v9)
                return v9

            kT_ready = 0
            p1_load(0)
            p1_load(1)
            for it in range(NT + 1):
                if it < NT:
                    p1_ln(it)
                    p1_txn(it)
                    p1_load(it + 2)
                    p1_kv(it)
                    p1_norm(it)
                if it >= 1:
                    kT_ready = p1_tk(it - 1)
            ckv_ready = vck_t[NT - 1]

            if PHASES <= 1:
                SQ.drain()
                GQ.drain()
            SQ.drain()
            GQ.drain()
            b.flush("P1")
        if PHASES <= 1:
            prep.close()
            attres.close()
            return nc

        with ExitStack() as ph:
            psb = lambda n, sh, dt: ph.enter_context(nc.sbuf_tensor(n, sh, dt))
            xnt_t = [psb(f"p2xnt{i}", [128, E], BF16) for i in range(2)]
            wqa_sb = psb("wqa_sb", [128, 16 * QLR], BF16)
            wqbpe_sb = psb("wqbpe_sb", [128, 12 * 1024], BF16)
            qa_f = psb("qa_f", [128, QLR], F32)
            qa_bf = [psb(f"qa_bf{i}", [128, QLR], BF16) for i in range(2)]
            qat_t = [psb(f"qat_t{i}", [128, 12 * 128], BF16) for i in range(2)]
            qpe_bf = psb("qpe_bf", [128, 1024], BF16)
            rs2 = psb("rs2", [128, 1], F32)
            vq2 = psb("vq2", [128, 1], F32)
            invq = psb("invq", [128, 1], F32)
            qm1 = psb("qm1", [128, 256], F32)
            qm2 = psb("qm2", [128, 256], F32)

            d_wqa = GQ.dma(_dma3(wqa_sb, wqa[:, :, :], 1536))
            d_wpe = GQ.dma(_dma3(wqbpe_sb, wqbpe[:, :, :], 1024))
            xnt_free2 = [None, None]
            qat_free = [None, None]
            qat_dma = [None] * NT
            qpeT_ready = 0

            d_xn_t = [None] * (NT + 1)
            t_qa_t = [None] * NT
            s4_t = [None] * NT
            v3_t = [None] * NT
            v4_t = [None] * NT
            t_pe_t = [None] * NT
            rope_v = [None] * NT
            t_tqa = [None] * NT
            t_tqpe = [None] * NT

            def stage_load(i):
                if i >= NT:
                    return
                b.waits("sync", [xnt_free2[i % 2], xnt_dma[i]])
                d_xn_t[i] = SQ.dma((lambda i: _dma3(
                    xnt_t[i % 2], xnt_scr[:, :, i * 128:(i + 1) * 128], 128))(i))

            def stage_qa(i):
                sl = i % 2
                b.waits("tensor", [d_wqa, d_xn_t[i],
                                   ps_free[0], ps_free[1], ps_free[2]])
                for lg in range(3):
                    for ec in range(16):
                        b.op("tensor", lambda e, sl=sl, lg=lg, ec=ec: e.matmul(
                            PS[lg][:, :], xnt_t[sl][:, ec * 128:(ec + 1) * 128],
                            wqa_sb[:, ec * QLR + lg * 512:ec * QLR + (lg + 1) * 512],
                            start=(ec == 0), stop=(ec == 15)),
                            T if ec == 15 else None)
                t_qa_t[i] = b.cnt[T]
                xnt_free2[sl] = (T, t_qa_t[i])

            def stage_norm(i):
                b.waits("scalar", [(T, t_qa_t[i]),
                                   (V, v3_t[i - 1]) if i > 0 else None])
                for lg in range(3):
                    b.op("scalar", lambda e, lg=lg: e.square(
                        qa_f[:, lg * 512:(lg + 1) * 512], PS[lg][:, :]),
                        Sc if lg == 2 else None)
                s1 = b.cnt[Sc]
                b.wait("vector", Sc, s1)
                v1 = b.op("vector", lambda e: e.reduce_sum(rs2[:, :], qa_f[:, :], AX.X), V)
                b.wait("vector", V, v1)
                v2 = b.op("vector", lambda e: e.tensor_scalar(
                    vq2[:, :], rs2[:, :], 1.0 / QLR, EPS_RMS, ALU.mult, ALU.add), V)
                b.wait("scalar", V, v2)
                s2 = b.op("scalar", lambda e: e.activation(invq[:, :], vq2[:, :], AF.Ln), Sc)
                b.wait("scalar", Sc, s2)
                s3 = b.op("scalar", lambda e: e.activation(
                    invq[:, :], invq[:, :], AF.Exp, scale=-0.5), Sc)
                b.wait("scalar", Sc, s3)
                for lg in range(3):
                    b.op("scalar", lambda e, lg=lg: e.activation(
                        qa_f[:, lg * 512:(lg + 1) * 512], PS[lg][:, :], AF.Copy,
                        scale=invq[:, :]), Sc if lg == 2 else None)
                s4_t[i] = b.cnt[Sc]
                for lg in range(3):
                    ps_free[lg] = (Sc, s4_t[i])
                b.waits("vector", [(Sc, s4_t[i]),
                                   (T, t_tqa[i - 2]) if i >= 2 else None])
                v3_t[i] = b.op("vector", lambda e, i=i: e.tensor_tensor(
                    qa_bf[i % 2][:, :], qa_f[:, :], lnq_rep[:, :], ALU.mult), V)

            def stage_tqa(i):
                sl = i % 2
                b.waits("tensor", [(V, v3_t[i]), pt_free[0], pt_free[1],
                                   qat_free[sl]])
                for lc in range(12):
                    pb = lc // 8
                    b.op("tensor", lambda e, lc=lc, pb=pb, i=i: e.matmul(
                        PT[pb][:, (lc % 8) * 128:((lc % 8) + 1) * 128],
                        qa_bf[i % 2][:, lc * 128:(lc + 1) * 128], ident[:, :],
                        is_transpose=True, start=(lc % 8 == 0),
                        stop=(lc in (7, 11))),
                        T if lc in (7, 11) else None)
                t_tqa[i] = b.cnt[T]
                b.wait("vector", T, t_tqa[i])
                b.op("vector", lambda e, sl=sl: e.tensor_copy(
                    qat_t[sl][:, 0:1024], PT[0][:, :]))
                v4_t[i] = b.op("vector", lambda e, sl=sl: e.tensor_copy(
                    qat_t[sl][:, 1024:1536], PT[1][:, 0:512]), V)
                pt_free[0] = (V, v4_t[i])
                pt_free[1] = (V, v4_t[i])

            def stage_spill(i):
                sl = i % 2
                b.wait("sync", V, v4_t[i])
                qat_dma[i] = SQ.dma((lambda sl, i: _dma3_out(
                    qat_scr[:, :, i * 128:(i + 1) * 128], qat_t[sl], 128))(sl, i))

            def stage_qpe(i):
                sl = i % 2
                b.waits("tensor", [d_wpe, (V, v4_t[i]), ps_free[3], ps_free[4]])
                for ng in range(2):
                    for lc in range(12):
                        b.op("tensor", lambda e, sl=sl, ng=ng, lc=lc: e.matmul(
                            PS[3 + ng][:, :], qat_t[sl][:, lc * 128:(lc + 1) * 128],
                            wqbpe_sb[:, lc * 1024 + ng * 512:lc * 1024 + (ng + 1) * 512],
                            start=(lc == 0), stop=(lc == 11)),
                            T if lc == 11 else None)
                t_pe_t[i] = b.cnt[T]
                qat_free[sl] = [(T, t_pe_t[i]), qat_dma[i]]

            def stage_rope(i):
                b.waits("vector", [(T, t_pe_t[i]),
                                   (T, t_tqpe[i - 1]) if i >= 1 else None])
                for hp in range(8):
                    bank = PS[3 + hp // 4]
                    o = (hp % 4) * 128
                    q0 = (hp % 2) * 128
                    b.op("vector", lambda e, bank=bank, o=o, tt=i, q0=q0: e.tensor_tensor(
                        qm1[:, q0:q0 + 128], bank[:, o:o + 128],
                        cosx2[:, tt * 128:(tt + 1) * 128], ALU.mult))
                    b.op("vector", lambda e, bank=bank, o=o, tt=i, q0=q0: e.tensor_tensor(
                        qm2[:, q0:q0 + 32], bank[:, o + 32:o + 64],
                        sinx2[:, tt * 128:tt * 128 + 32], ALU.mult))
                    b.op("vector", lambda e, bank=bank, o=o, tt=i, q0=q0: e.tensor_tensor(
                        qm2[:, q0 + 32:q0 + 64], bank[:, o:o + 32],
                        sinx2[:, tt * 128 + 32:tt * 128 + 64], ALU.mult))
                    b.op("vector", lambda e, bank=bank, o=o, tt=i, q0=q0: e.tensor_tensor(
                        qm2[:, q0 + 64:q0 + 96], bank[:, o + 96:o + 128],
                        sinx2[:, tt * 128 + 64:tt * 128 + 96], ALU.mult))
                    v5 = b.op("vector", lambda e, bank=bank, o=o, tt=i, q0=q0: e.tensor_tensor(
                        qm2[:, q0 + 96:q0 + 128], bank[:, o + 64:o + 96],
                        sinx2[:, tt * 128 + 96:tt * 128 + 128], ALU.mult), V)
                    b.wait("vector", V, v5)
                    b.op("vector", lambda e, hp=hp, q0=q0: e.tensor_tensor(
                        qpe_bf[:, hp * 128:(hp + 1) * 128], qm1[:, q0:q0 + 128],
                        qm2[:, q0:q0 + 128], ALU.add),
                        V if hp == 7 else None)
                rope_v[i] = b.cnt[V]
                ps_free[3] = (V, rope_v[i])
                ps_free[4] = (V, rope_v[i])

            def stage_tqpe(i):
                nonlocal qpeT_ready
                b.waits("tensor", [(V, rope_v[i]), pt_free[0]])
                for hp in range(8):
                    b.op("tensor", lambda e, hp=hp: e.transpose(
                        PT[0][:, hp * 128:(hp + 1) * 128],
                        qpe_bf[:, hp * 128:(hp + 1) * 128], ident[:, :]),
                        T if hp == 7 else None)
                t_tqpe[i] = b.cnt[T]
                b.wait("vector", T, t_tqpe[i])
                for hp in range(8):
                    b.op("vector", lambda e, hp=hp, tt=i: e.tensor_copy(
                        qpeT[hp][:, tt * 128:(tt + 1) * 128],
                        PT[0][:, hp * 128:(hp + 1) * 128]),
                        V if hp == 7 else None)
                qpeT_ready = b.cnt[V]
                pt_free[0] = (V, qpeT_ready)

            stage_load(0)
            for it in range(NT + 2):
                if it < NT:
                    stage_qa(it)
                    stage_load(it + 1)
                    stage_norm(it)
                if it >= 2 and it - 2 < NT:
                    stage_tqpe(it - 2)
                if it >= 1 and it - 1 < NT:
                    stage_tqa(it - 1)
                    stage_spill(it - 1)
                    stage_qpe(it - 1)
                    stage_rope(it - 1)

            if PHASES <= 2:
                SQ.drain()
                GQ.drain()
            SQ.drain()
            GQ.drain()
            b.flush("P2")
        if PHASES <= 2:
            prep.close()
            attres.close()
            return nc

        prep.close()

        with ExitStack() as ph:
            psb = lambda n, sh, dt: ph.enter_context(nc.sbuf_tensor(n, sh, dt))
            qat_B = psb("qat_B", [128, 12 * 512], BF16)
            wukt_sb = [psb(f"wukt{i}", [128, 512], BF16) for i in range(2)]
            wnope_sb = [psb(f"wnope{i}", [128, 12 * 128], BF16) for i in range(2)]
            wuv_sb = [psb(f"wuv{i}", [128, 4 * 128], BF16) for i in range(2)]
            wo_sb = [psb(f"wo{i}", [128, H * 512], BF16) for i in range(2)]
            ptasm = psb("ptasm", [128, 16 * 512], BF16)
            p_nat = [psb(f"p_nat{i}", [128, S], BF16) for i in range(2)]
            qf = [psb(f"qf{i}", [128, 4 * 512], BF16) for i in range(2)]
            qnope_sb = [psb(f"qnope{i}", [128, 512], BF16) for i in range(2)]
            o_sb = psb("o_sb", [128, 4 * 512], BF16)
            t1_sb = psb("t1_sb", [128, H * 512], BF16)
            s_stage = psb("s_stage", [128, S], F32)
            y_sb = [psb(f"y_sb{i}", [128, 512], F32) for i in range(2)]
            mx_part = psb("mx_part", [128, 4], F32)
            mx_row = psb("mx_row", [128, 1], F32)
            e_bias = psb("e_bias", [128, 1], F32)
            sm_sums = psb("sm_sums", [128, 4], F32)
            sm_tot = psb("sm_tot", [128, 1], F32)
            sm_inv = psb("sm_inv", [128, 1], F32)

            b.waits("vector", [d_mask])

            conv_sb = [psb(f"conv{i}", [128, 2048], BF16) for i in range(3)]
            conv_free = [None] * 3
            conv_units = ([("g", oc) for oc in range(128)]
                          + [("d", u) for u in range(64)])
            conv_i = [0]

            def emit_conversions(n):
                for _ in range(n):
                    if conv_i[0] >= len(conv_units):
                        return
                    kind, u = conv_units[conv_i[0]]
                    slot = conv_i[0] % 3
                    conv_i[0] += 1
                    b.waits("gpsimd", [conv_free[slot]])
                    if kind == "g":
                        tok = GQ.dma(_dma3(conv_sb[slot], wgup[u, :, :, :], 128))
                        b.waits("sync", [tok])
                        st = SQ.dma((lambda u, slot: _dma3_out(
                            wgup_bf[u, :, :, :], conv_sb[slot], 128))(u, slot))
                    else:
                        eg, ic0 = u % 4, (u // 4) * 4
                        tok = GQ.dma((lambda eg, ic0, slot: _dma3(
                            conv_sb[slot], wdown[eg, ic0:ic0 + 4, :, :], 512))(eg, ic0, slot))
                        b.waits("sync", [tok])
                        st = SQ.dma((lambda eg, ic0, slot: _dma3_out(
                            wdown_bf[eg, ic0:ic0 + 4, :, :], conv_sb[slot], 512))(eg, ic0, slot))
                    conv_free[slot] = st

            qat_B_free = [None]
            wnope_free = [None, None]
            wuv_free = [None, None]
            wo_free = [None, None]
            ptasm_free = None
            p_nat_free = [None, None]
            qf_free = [None, None]
            qnope_free = [None, None]
            o_sb_free = None
            t1_free = None
            y_free = [None, None]
            ya_dma = None
            ya_done = [None] * NB
            d_nope = [None, None]
            d_uv = [None, None]
            d_wo = [None, None]

            def load_nope(h, slot):
                b.waits("gpsimd", [wnope_free[slot]])
                d_nope[slot] = GQ.dma((lambda h, slot: _dma3(wnope_sb[slot], wqbnope[h, :, :, :], 128))(h, slot))

            wukt_free = [None, None]
            d_ukt = [None, None]

            def load_ukt(h, slot):
                b.waits("gpsimd", [wukt_free[slot]])
                d_ukt[slot] = GQ.dma((lambda h, slot: e_dma_ukt(h, slot))(h, slot))

            def e_dma_ukt(h, slot):
                return lambda e: e.dma_start(wukt_sb[slot][:, :], wukt[h, :, :])

            def load_uv(h, slot):
                b.waits("gpsimd", [wuv_free[slot]])
                d_uv[slot] = GQ.dma((lambda h, slot: _dma3(wuv_sb[slot], wuv[h, :, :, :], 128))(h, slot))

            def load_wo(eg, slot):
                b.waits("gpsimd", [wo_free[slot]])
                d_wo[slot] = GQ.dma((lambda eg, slot: _dma3(wo_sb[slot], wo[eg, :, :, :], 512))(eg, slot))

            load_nope(0, 0)
            load_uv(0, 0)
            load_ukt(0, 0)

            d_qat = [None]
            for B in range(NB):
                nkB = 4 * (B + 1)
                b.waits("sync", [qat_B_free[0]])
                b.waits("sync", [qat_dma[4 * B + j] for j in range(4)])
                d_qat[0] = SQ.dma((lambda B: _dma3(
                    qat_B, qat_scr[:, :, B * 512:(B + 1) * 512], 512))(B))
                if B == 0:
                    load_wo(0, 0)
                def emit_qnope(h2, B=B):
                    hs2 = h2 % 2
                    b.waits("tensor", [d_nope[hs2], d_qat[0], ps_free[5]])
                    for lc in range(12):
                        b.op("tensor", lambda e, hs2=hs2, lc=lc: e.matmul(
                            PS[5][:, :], wnope_sb[hs2][:, lc * 128:(lc + 1) * 128],
                            qat_B[:, lc * 512:(lc + 1) * 512],
                            start=(lc == 0), stop=(lc == 11)),
                            T if lc == 11 else None)
                    t_qn = b.cnt[T]
                    if h2 < 15:
                        load_nope(h2 + 1, 1 - hs2)
                    elif B < NB - 1:
                        load_nope(0, 1 - hs2)
                    wnope_free[hs2] = (T, t_qn)
                    b.waits("scalar", [(T, t_qn), qnope_free[hs2]])
                    s_qn = b.op("scalar", lambda e, hs2=hs2: e.copy(
                        qnope_sb[hs2][:, :], PS[5][:, :]), Sc)
                    ps_free[5] = (Sc, s_qn)
                    return s_qn

                s_qn_tok = [None, None]
                for h in range(16):
                    hs = h % 2
                    emit_conversions(3)
                    if h == 0:
                        s_qn_tok[0] = emit_qnope(0)
                    s_qn = s_qn_tok[hs]
                    qab_banks = (3, 4, 5, 0)
                    b.waits("tensor", [d_ukt[hs], (Sc, s_qn)]
                            + [ps_free[j] for j in qab_banks])
                    for rc in range(4):
                        b.op("tensor", lambda e, hs=hs, rc=rc, qb=qab_banks: e.matmul(
                            PS[qb[rc]][:, :],
                            wukt_sb[hs][:, rc * 128:(rc + 1) * 128],
                            qnope_sb[hs][:, :], start=True, stop=True),
                            T if rc == 3 else None)
                    t_qab = b.cnt[T]
                    wukt_free[hs] = (T, t_qab)
                    if h < 15:
                        load_ukt(h + 1, 1 - hs)
                    elif B < NB - 1:
                        load_ukt(0, 1 - hs)
                    qnope_free[hs] = (T, t_qab)
                    b.waits("vector", [(T, t_qab), qf_free[hs]])
                    for rc in range(4):
                        b.op("vector", lambda e, hs=hs, rc=rc, qb=qab_banks: e.tensor_copy(
                            qf[hs][:, rc * 512:(rc + 1) * 512], PS[qb[rc]][:, :]),
                            V if rc == 3 else None)
                    v_qf = b.cnt[V]
                    for j in qab_banks:
                        ps_free[j] = (V, v_qf)

                    pt_evac_v = 0
                    last_S_T = 0
                    sbank = 0
                    prev_exp = None
                    pend_tr = None

                    def emit_transposes(i, n_k, v_pn):
                        nonlocal pt_evac_v
                        b.wait("tensor", V, v_pn)
                        t_tr_done = 0
                        for kc in range(n_k):
                            pb = (kc // 8) % 2
                            slot = kc % 8
                            if slot == 0:
                                b.waits("tensor", [pt_free[pb]])
                            b.op("tensor", lambda e, pb=pb, slot=slot, kc=kc, i=i,
                                 last=(slot == 7 or kc == n_k - 1): e.matmul(
                                PT[pb][:, slot * 128:(slot + 1) * 128],
                                p_nat[i % 2][:, kc * 128:(kc + 1) * 128], ident[:, :],
                                is_transpose=True, start=(slot == 0), stop=last),
                                T if (slot == 7 or kc == n_k - 1) else None)
                            if slot == 7 or kc == n_k - 1:
                                t_tr_done = b.cnt[T]
                                nsl = slot + 1
                                base = kc - slot
                                b.waits("vector", [(T, t_tr_done), ptasm_free])
                                b.op("vector", lambda e, pb=pb, nsl=nsl, base=base, i=i:
                                     e.tensor_copy(
                                         ptasm[:, :].rearrange(
                                             "p (a c) -> p a c", c=512)[
                                             :, base:base + nsl, i * 128:(i + 1) * 128],
                                         PT[pb][:, :].rearrange(
                                             "p (a c) -> p a c", c=128)[:, 0:nsl, :]),
                                     V)
                                pt_free[pb] = (V, b.cnt[V])
                        p_nat_free[i % 2] = (T, t_tr_done)
                        pt_evac_v = b.cnt[V]

                    for i in range(4):
                        qi = 4 * B + i
                        n_k = qi + 1
                        G = (n_k + 3) // 4
                        s_ev = {}
                        for g in range(G):
                            w = (min(4 * g + 4, n_k) - 4 * g) * 128
                            pk = 1 + (sbank % 3)
                            sbank += 1
                            bank = PS[pk]
                            b.waits("tensor", [(V, v_qf), ps_free[pk],
                                               (V, qpeT_ready), (V, kT_ready)])
                            for cc in range(5):
                                if cc < 4:
                                    def lhs(e, hs=hs, cc=cc, i=i):
                                        return qf[hs][:, cc * 512 + i * 128:cc * 512 + (i + 1) * 128]

                                    def rhs(e, cc=cc, g=g, w=w):
                                        return kT[cc][:, 4 * g * 128:4 * g * 128 + w]
                                else:
                                    def lhs(e, h=h, B=B, i=i):
                                        return qpeT[h // 2][:,
                                                            B * 512 + i * 128:B * 512 + (i + 1) * 128]

                                    def rhs(e, h=h, g=g, w=w):
                                        kk = kTpe_lo if h % 2 == 0 else kTpe_hi
                                        return kk[:, 4 * g * 128:4 * g * 128 + w]
                                b.op("tensor", lambda e, bank=bank, w=w, cc=cc,
                                     lhs=lhs, rhs=rhs: e.matmul(
                                         bank[:, 0:w], lhs(e), rhs(e),
                                         start=(cc == 0), stop=(cc == 4)),
                                     T if cc == 4 else None)
                            t_sg = b.cnt[T]
                            b.wait("scalar", T, t_sg)
                            if prev_exp is not None:
                                b.wait("scalar", Sc, prev_exp)
                            s_ev[g] = b.op("scalar", lambda e, g=g, w=w, bank=bank: e.copy(
                                s_stage[:, 4 * g * 128:4 * g * 128 + w], bank[:, 0:w]), Sc)
                            ps_free[pk] = (Sc, s_ev[g])
                        last_S_T = b.cnt[T]
                        gl = G - 1
                        doff = (qi) * 128
                        b.wait("vector", Sc, s_ev[gl])
                        v_mask = b.op("vector", lambda e, doff=doff: e.tensor_tensor(
                            s_stage[:, doff:doff + 128], s_stage[:, doff:doff + 128],
                            maskneg[:, :], ALU.add), V)
                        for g in range(G):
                            w = (min(4 * g + 4, n_k) - 4 * g) * 128
                            b.wait("vector", Sc, s_ev[g])
                            if g == gl:
                                b.wait("vector", V, v_mask)
                            b.op("vector", lambda e, g=g, w=w: e.reduce_max(
                                mx_part[:, g:g + 1], s_stage[:, 4 * g * 128:4 * g * 128 + w],
                                AX.X), V if g == G - 1 else None)
                        v_pm = b.cnt[V]
                        b.wait("vector", V, v_pm)
                        v_cm = b.op("vector", lambda e, G=G: e.reduce_max(
                            mx_row[:, :], mx_part[:, 0:G], AX.X), V)
                        b.wait("vector", V, v_cm)
                        v_bias = b.op("vector", lambda e: e.tensor_scalar(
                            e_bias[:, :], mx_row[:, :], -SM, None, ALU.mult), V)
                        b.waits("scalar", [(V, v_bias), p_nat_free[i % 2]])
                        for g in range(G):
                            w = (min(4 * g + 4, n_k) - 4 * g) * 128
                            b.op("scalar", lambda e, g=g, w=w, i=i: e.activation(
                                p_nat[i % 2][:, 4 * g * 128:4 * g * 128 + w],
                                s_stage[:, 4 * g * 128:4 * g * 128 + w],
                                AF.Exp, bias=e_bias[:, :], scale=SM,
                                accum_out=sm_sums[:, g:g + 1]),
                                Sc if g == G - 1 else None)
                        s_exp = b.cnt[Sc]
                        prev_exp = s_exp
                        b.wait("vector", Sc, s_exp)
                        v_st = b.op("vector", lambda e, G=G: e.reduce_sum(
                            sm_tot[:, :], sm_sums[:, 0:G], AX.X), V)
                        b.wait("vector", V, v_st)
                        v_si = b.op("vector", lambda e: e.reciprocal(
                            sm_inv[:, :], sm_tot[:, :]), V)
                        b.wait("vector", V, v_si)
                        v_pn = b.op("vector", lambda e, n_k=n_k, i=i: e.tensor_scalar(
                            p_nat[i % 2][:, 0:n_k * 128], p_nat[i % 2][:, 0:n_k * 128],
                            sm_inv[:, :], None, ALU.mult), V)
                        if pend_tr is not None:
                            emit_transposes(*pend_tr)
                        pend_tr = (i, n_k, v_pn)
                    if h < 15:
                        s_qn_tok[1 - hs] = emit_qnope(h + 1)
                    emit_transposes(*pend_tr)
                    qf_free[hs] = (T, last_S_T)

                    o_banks = (3, 4, 5, 0)
                    b.waits("tensor", [(V, pt_evac_v), (V, ckv_ready),
                                       t1_free if h == 0 else None]
                            + [ps_free[j] for j in o_banks])
                    for kc in range(nkB):
                        j0 = max(0, kc - 4 * B)
                        for rc in range(4):
                            b.op("tensor", lambda e, kc=kc, j0=j0, rc=rc, ob=o_banks,
                                 nkB=nkB: e.matmul(
                                     PS[ob[rc]][:, j0 * 128:512],
                                     ckv_nat[:, kc * 512 + rc * 128:kc * 512 + (rc + 1) * 128],
                                     ptasm[:, kc * 512 + j0 * 128:(kc + 1) * 512],
                                     start=(kc == 0), stop=(kc == nkB - 1)),
                                 T if (kc == nkB - 1 and rc == 3) else None)
                    t_o = b.cnt[T]
                    ptasm_free = (T, t_o)
                    b.waits("vector", [(T, t_o), o_sb_free])
                    b.waits("scalar", [(T, t_o), o_sb_free])
                    for rc in range(2):
                        b.op("vector", lambda e, rc=rc, ob=o_banks: e.tensor_copy(
                            o_sb[:, rc * 512:(rc + 1) * 512], PS[ob[rc]][:, :]),
                            V if rc == 1 else None)
                    v_o = b.cnt[V]
                    for rc in range(2, 4):
                        b.op("scalar", lambda e, rc=rc, ob=o_banks: e.copy(
                            o_sb[:, rc * 512:(rc + 1) * 512], PS[ob[rc]][:, :]),
                            Sc if rc == 3 else None)
                    s_o = b.cnt[Sc]
                    ps_free[o_banks[0]] = (V, v_o)
                    ps_free[o_banks[1]] = (V, v_o)
                    ps_free[o_banks[2]] = (Sc, s_o)
                    ps_free[o_banks[3]] = (Sc, s_o)
                    b.waits("tensor", [(V, v_o), (Sc, s_o), d_uv[hs], ps_free[1]])
                    for rc in range(4):
                        b.op("tensor", lambda e, hs=hs, rc=rc: e.matmul(
                            PS[1][:, :], wuv_sb[hs][:, rc * 128:(rc + 1) * 128],
                            o_sb[:, rc * 512:(rc + 1) * 512],
                            start=(rc == 0), stop=(rc == 3)),
                            T if rc == 3 else None)
                    t_t1 = b.cnt[T]
                    o_sb_free = (T, t_t1)
                    wuv_free[hs] = (T, t_t1)
                    if h < 15:
                        load_uv(h + 1, 1 - hs)
                    elif B < NB - 1:
                        load_uv(0, 1 - hs)
                    b.wait("scalar", T, t_t1)
                    s_t1 = b.op("scalar", lambda e, h=h: e.copy(
                        t1_sb[:, h * 512:(h + 1) * 512], PS[1][:, :]), Sc)
                    ps_free[1] = (Sc, s_t1)
                t1_all_ready = b.cnt[Sc]

                for eg in range(4):
                    egs = eg % 2
                    if eg < 3:
                        load_wo(eg + 1, 1 - egs)
                    elif B < NB - 1:
                        load_wo(0, 1 - egs)
                    for tt4 in range(4):
                        bank = PS[1 + (tt4 % 2)]
                        b.waits("tensor", [d_wo[egs], (Sc, t1_all_ready),
                                           ps_free[1 + (tt4 % 2)]])
                        for hh in range(16):
                            b.op("tensor", lambda e, hh=hh, tt4=tt4, egs=egs, bank=bank:
                                 e.matmul(
                                     bank[:, :],
                                     t1_sb[:, hh * 512 + tt4 * 128:hh * 512 + (tt4 + 1) * 128],
                                     wo_sb[egs][:, hh * 512:(hh + 1) * 512],
                                     start=(hh == 0), stop=(hh == 15)),
                                 T if hh == 15 else None)
                        t_y = b.cnt[T]
                        if tt4 == 3:
                            wo_free[egs] = (T, t_y)
                        ysl = tt4 % 2
                        b.waits("scalar", [(T, t_y), y_free[ysl]])
                        s_y = b.op("scalar", lambda e, ysl=ysl, bank=bank: e.copy(
                            y_sb[ysl][:, :], bank[:, :]), Sc)
                        ps_free[1 + (tt4 % 2)] = (Sc, s_y)
                        b.wait("sync", Sc, s_y)
                        ya_dma = SQ.dma(lambda e, B=B, tt4=tt4, eg=eg, ysl=ysl:
                                      e.dma_start(
                                          ya_scr[B * 512 + tt4 * 128:B * 512 + (tt4 + 1) * 128,
                                                 eg * 512:(eg + 1) * 512],
                                          y_sb[ysl][:, :]))
                        y_free[ysl] = ya_dma
                qat_B_free[0] = (T, b.cnt[T])
                ya_done[B] = ya_dma
                t1_free = (T, b.cnt[T])

            if PHASES <= 3:
                SQ.drain()
                GQ.drain()
            SQ.drain()
            GQ.drain()
            b.flush("ATT")
        attres.close()
        if PHASES <= 3:
            return nc

        with ExitStack() as ph:
            psb = lambda n, sh, dt: ph.enter_context(nc.sbuf_tensor(n, sh, dt))
            xnt_g = [psb(f"fx{i}", [128, 16 * 512], BF16) for i in range(2)]
            h_sb = psb("h_sb", [128, 64 * 512], BF16)
            wg_sb = [psb(f"wg{i}", [128, 16 * 128], BF16) for i in range(4)]
            wd_sb = [psb(f"wd{i}", [128, 4 * 512], BF16) for i in range(2)]
            sil_sb = [psb(f"sil{i}", [128, 512], F32) for i in range(2)]
            xr_sb = psb("xr_sb", [128, 16 * 512], F32)
            yr_sb = psb("yr_sb", [128, 16 * 512], F32)
            ydn_sb = psb("ydn_sb", [128, 4 * 512], F32)
            out_sb = [psb(f"os{i}", [128, 512], F32) for i in range(2)]

            xnt_g_free = [None, None]
            wg_free = [None] * 4
            wd_free = [None, None]
            sil_free = [None, None]
            xr_free = None
            yr_free = None
            ydn_free = None
            d_xr = None
            d_yr = None
            out_free = [None, None]
            h_free_T = 0
            d_g = [None] * 4
            d_d = [None, None]
            out_dma = None

            def load_gate_tile(j, kind, slot):
                oc = j + (64 if kind else 0)
                b.waits("sync", [wg_free[slot]])
                d_g[slot] = SQ.dma((lambda oc, slot: _dma3(wg_sb[slot], wgup_bf[oc, :, :, :], 128))(oc, slot))

            def load_down(eg, ic0, slot):
                b.waits("sync", [wd_free[slot]])
                d_d[slot] = SQ.dma((lambda eg, ic0, slot: _dma3(wd_sb[slot], wdown_bf[eg, ic0:ic0 + 4, :, :], 512))(eg, ic0, slot))

            for g in range(4):
                gs = g % 2
                b.waits("sync", [xnt_g_free[gs]])
                b.waits("sync", list(xnt_dma))
                d_xg = SQ.dma((lambda g, gs: _dma3(xnt_g[gs], xnt_scr[:, :, g * 512:(g + 1) * 512], 512))(g, gs))
                if g == 0:
                    load_gate_tile(0, 0, 0)
                    load_gate_tile(0, 1, 1)
                for j in range(64):
                    s_g = 2 * j % 4
                    s_u = (2 * j + 1) % 4
                    if j < 63:
                        load_gate_tile(j + 1, 0, (2 * j + 2) % 4)
                        load_gate_tile(j + 1, 1, (2 * j + 3) % 4)
                    elif g < 3:
                        load_gate_tile(0, 0, (2 * j + 2) % 4)
                        load_gate_tile(0, 1, (2 * j + 3) % 4)
                    if j == 8:
                        b.waits("gpsimd", [xr_free, yr_free, ya_done[g]])
                        d_res = []
                        for tt4 in range(4):
                            d_res.append(GQ.dma(lambda e, g=g, tt4=tt4: e.dma_start(
                                xr_sb[:, tt4 * 2048:(tt4 + 1) * 2048],
                                x_d[g * 512 + tt4 * 128:g * 512 + (tt4 + 1) * 128, :])))
                            d_res.append(GQ.dma(lambda e, g=g, tt4=tt4: e.dma_start(
                                yr_sb[:, tt4 * 2048:(tt4 + 1) * 2048],
                                ya_scr[g * 512 + tt4 * 128:g * 512 + (tt4 + 1) * 128, :])))
                    pg, pu = PS[(j % 2) * 2], PS[(j % 2) * 2 + 1]
                    b.waits("tensor", [d_g[s_g], d_xg,
                                       ps_free[(j % 2) * 2]])
                    for ec in range(16):
                        b.op("tensor", lambda e, gs=gs, ec=ec, s_g=s_g, pg=pg: e.matmul(
                            pg[:, :], wg_sb[s_g][:, ec * 128:(ec + 1) * 128],
                            xnt_g[gs][:, ec * 512:(ec + 1) * 512],
                            start=(ec == 0), stop=(ec == 15)),
                            T if ec == 15 else None)
                    t_g = b.cnt[T]
                    b.waits("tensor", [d_g[s_u], ps_free[(j % 2) * 2 + 1]])
                    for ec in range(16):
                        b.op("tensor", lambda e, gs=gs, ec=ec, s_u=s_u, pu=pu: e.matmul(
                            pu[:, :], wg_sb[s_u][:, ec * 128:(ec + 1) * 128],
                            xnt_g[gs][:, ec * 512:(ec + 1) * 512],
                            start=(ec == 0), stop=(ec == 15)),
                            T if ec == 15 else None)
                    t_u = b.cnt[T]
                    wg_free[s_g] = (T, t_g)
                    wg_free[s_u] = (T, t_u)
                    ssl = j % 2
                    b.waits("scalar", [(T, t_g), sil_free[ssl]])
                    s_si = b.op("scalar", lambda e, pg=pg, ssl=ssl: e.activation(
                        sil_sb[ssl][:, :], pg[:, :], AF.Silu), Sc)
                    ps_free[(j % 2) * 2] = (Sc, s_si)
                    b.waits("vector", [(Sc, s_si), (T, t_u),
                                       (T, h_free_T) if h_free_T else None])
                    v_h = b.op("vector", lambda e, j=j, ssl=ssl, pu=pu: e.tensor_tensor(
                        h_sb[:, j * 512:(j + 1) * 512], sil_sb[ssl][:, :], pu[:, :],
                        ALU.mult), V)
                    sil_free[ssl] = (V, v_h)
                    ps_free[(j % 2) * 2 + 1] = (V, v_h)
                h_ready = b.cnt[V]
                xnt_g_free[gs] = (T, b.cnt[T])

                for eg in range(4):
                    if g == 0 and eg == 0:
                        load_down(0, 0, 0)
                    banks = [PS[2 + k] for k in range(4)]
                    b.waits("tensor", [(V, h_ready)]
                            + [ps_free[2 + k] for k in range(4)])
                    for ic in range(64):
                        isl = (ic // 4) % 2
                        if ic % 4 == 0:
                            nic = ic + 4
                            neg, ng2 = eg, g
                            if nic >= 64:
                                nic = 0
                                neg = eg + 1
                                if neg >= 4:
                                    neg, ng2 = 0, g + 1
                            if ng2 < 4:
                                load_down(neg, nic, 1 - isl)
                        b.waits("tensor", [d_d[isl]])
                        for tt4 in range(4):
                            b.op("tensor", lambda e, ic=ic, tt4=tt4, isl=isl,
                                 banks=banks: e.matmul(
                                     banks[tt4][:, :],
                                     h_sb[:, ic * 512 + tt4 * 128:ic * 512 + (tt4 + 1) * 128],
                                     wd_sb[isl][:, (ic % 4) * 512:(ic % 4 + 1) * 512],
                                     start=(ic == 0), stop=(ic == 63)),
                                 T if (tt4 == 3 and ic % 4 == 3) else None)
                        if ic % 4 == 3:
                            wd_free[isl] = (T, b.cnt[T])
                    t_dn = b.cnt[T]
                    b.waits("scalar", [(T, t_dn), ydn_free])
                    for tt4 in range(4):
                        b.op("scalar", lambda e, tt4=tt4, banks=banks: e.copy(
                            ydn_sb[:, tt4 * 512:(tt4 + 1) * 512],
                            banks[tt4][:, :]), Sc if tt4 == 3 else None)
                    s_dn = b.cnt[Sc]
                    for k in range(4):
                        ps_free[2 + k] = (Sc, s_dn)
                    for tt4 in range(4):
                        osl = tt4 % 2
                        b.waits("vector", d_res + [(Sc, s_dn),
                                           out_free[osl]])
                        v_o1 = b.op("vector", lambda e, tt4=tt4, eg=eg, osl=osl:
                                    e.tensor_tensor(
                                        out_sb[osl][:, :],
                                        xr_sb[:, tt4 * 2048 + eg * 512:tt4 * 2048 + (eg + 1) * 512],
                                        yr_sb[:, tt4 * 2048 + eg * 512:tt4 * 2048 + (eg + 1) * 512],
                                        ALU.add), V)
                        b.wait("vector", V, v_o1)
                        v_ob = b.op("vector", lambda e, osl=osl, tt4=tt4:
                                    e.tensor_tensor(
                                        out_sb[osl][:, :], out_sb[osl][:, :],
                                        ydn_sb[:, tt4 * 512:(tt4 + 1) * 512],
                                        ALU.add), V)
                        b.wait("sync", V, v_ob)
                        out_dma = SQ.dma(lambda e, g=g, tt4=tt4, eg=eg, osl=osl:
                                       e.dma_start(
                                           out_d[g * 512 + tt4 * 128:g * 512 + (tt4 + 1) * 128,
                                                 eg * 512:(eg + 1) * 512],
                                           out_sb[osl][:, :]))
                        out_free[osl] = out_dma
                    v_cmb = b.cnt[V]
                    ydn_free = (V, v_cmb)
                    if eg == 3:
                        xr_free = (V, v_cmb)
                        yr_free = (V, v_cmb)
                h_free_T = b.cnt[T]

            b.waits("sync", [out_dma])
            SQ.drain()
            GQ.drain()
            b.flush("FFN")

    return nc


def _host_inputs(inp):
    x = np.ascontiguousarray(np.asarray(inp["L0_attn_x"], dtype=np.float32))
    w_q_a = np.asarray(inp["L0_attn_w_q_a"], dtype=np.float32)
    lnq = np.asarray(inp["L0_attn_q_a_ln_w"], dtype=np.float32)
    w_q_b = np.asarray(inp["L0_attn_w_q_b"], dtype=np.float32)
    w_kv_a = np.asarray(inp["L0_attn_w_kv_a"], dtype=np.float32)
    lnkv = np.asarray(inp["L0_attn_kv_a_ln_w"], dtype=np.float32)
    w_kv_b = np.asarray(inp["L0_attn_w_kv_b"], dtype=np.float32)
    w_o = np.asarray(inp["L0_attn_w_o"], dtype=np.float32)
    w_gup = np.asarray(inp["L0_ffn_w_gate_up"], dtype=np.float32)
    w_down = np.asarray(inp["L0_ffn_w_down"], dtype=np.float32)

    com = {}
    com["wqa"] = np.ascontiguousarray(w_q_a.reshape(16, 128, QLR))
    com["wkva"] = np.ascontiguousarray(w_kv_a.reshape(16, 128, R + DR))
    qb = w_q_b.reshape(QLR, H, DN + DR)
    com["wqbpe"] = np.ascontiguousarray(
        qb[:, :, DN:].reshape(QLR, H * DR).reshape(12, 128, H * DR))
    com["wqbnope"] = np.ascontiguousarray(
        qb[:, :, :DN].reshape(12, 128, H, DN).transpose(2, 0, 1, 3))
    kvb = w_kv_b.reshape(R, H, DN + DV)
    com["wukt"] = np.ascontiguousarray(kvb[:, :, :DN].transpose(1, 2, 0))
    com["wuv"] = np.ascontiguousarray(
        kvb[:, :, DN:].reshape(4, 128, H, DV).transpose(2, 0, 1, 3))
    com["wo"] = np.ascontiguousarray(
        w_o.reshape(H, 128, 4, 512).transpose(2, 0, 1, 3))
    com["wgup"] = np.ascontiguousarray(
        w_gup.reshape(128, 128, 16, 128).transpose(0, 2, 3, 1))
    com["wdown"] = np.ascontiguousarray(
        w_down.reshape(4, 512, 64, 128).transpose(0, 2, 3, 1))
    com["lnq"] = np.ascontiguousarray(lnq.reshape(1, QLR))
    com["lnkv"] = np.ascontiguousarray(lnkv.reshape(1, R))

    t = np.arange(S, dtype=np.float64)
    inv_freq = 1.0 / (10000.0 ** (np.arange(0, DR, 2, dtype=np.float64) / DR))
    ang = t[:, None] * inv_freq[None, :]
    cos32 = np.cos(ang).astype(np.float32)
    sin32 = np.sin(ang).astype(np.float32)
    cos64 = np.concatenate([cos32, cos32], axis=1)
    sinp = np.concatenate([-sin32, sin32], axis=1)
    com["cosx2"] = np.ascontiguousarray(
        np.concatenate([cos64, cos64], axis=1).reshape(16, 128, 128))
    com["sinx2"] = np.ascontiguousarray(
        np.concatenate([sinp, sinp], axis=1).reshape(16, 128, 128))
    com["ident"] = np.eye(128, dtype=np.float32)
    m = np.zeros((128, 128), dtype=np.float32)
    m[np.triu_indices(128, 1)] = NEG
    com["maskneg"] = m
    com["ones1"] = np.ones((1, 128), dtype=np.float32)

    maps = []
    for c in range(8):
        d = dict(com)
        d["x"] = np.ascontiguousarray(x[c])
        maps.append(d)
    return maps


def kernel(**inputs):
    maps = _host_inputs(inputs)
    nc = build_nc()
    res = run_bass_kernel_spmd(nc, maps, core_ids=list(range(8)))
    return np.stack([np.asarray(r["out"], dtype=np.float32) for r in res.results], axis=0)
